# version 02
# baseline (speedup 1.0000x reference)
from contextlib import ExitStack

import numpy as np
import ml_dtypes
import concourse.bass as bass
import concourse.mybir as mybir
from concourse.bass_utils import run_bass_kernel_spmd

F32 = mybir.dt.float32
BF16 = mybir.dt.bfloat16
AF = mybir.ActivationFunctionType
ALU = mybir.AluOpType
ts = bass.ts

NCORE = 8
B, S, DM = 2, 8192, 2048
DI = 4096
DS = 16
TOK = (B * S) // NCORE
TOKH = TOK + 3
DSL = DI // NCORE


def new_nc():
    return bass.Bass("TRN2", target_bir_lowering=False, debug=False)


def _f32c(a):
    return np.ascontiguousarray(np.asarray(a), dtype=np.float32)



def build_a1():
    nc = new_nc()
    xf = nc.dram_tensor("xf", [DM, TOKH], F32, kind="ExternalInput").ap()
    w_in_t = nc.dram_tensor("w_in_t", [DM, 2 * DI], F32, kind="ExternalInput").ap()
    conv_w = nc.dram_tensor("conv_w", [DI, 4], F32, kind="ExternalInput").ap()
    conv_b = nc.dram_tensor("conv_b", [DI, 1], F32, kind="ExternalInput").ap()
    w_x_t = nc.dram_tensor("w_x_t", [DI, 33], F32, kind="ExternalInput").ap()
    dt_bias = nc.dram_tensor("dt_bias", [128, DI // 128], F32, kind="ExternalInput").ap()
    idn = nc.dram_tensor("idn", [128, 128], F32, kind="ExternalInput").ap()

    xc_f = nc.dram_tensor("xc_f", [DI, TOK], F32, kind="ExternalOutput").ap()
    zs_f = nc.dram_tensor("zs_f", [DI, TOK], F32, kind="ExternalOutput").ap()
    dtbc_t = nc.dram_tensor("dtbc_t", [33, TOK], F32, kind="ExternalOutput").ap()

    KT = DM // 128
    XT = DI // 128
    MT = 2 * XT
    NCH = [(0, 512), (512, 512), (1024, 512), (1536, 512), (2048, 3)]
    NCHN = len(NCH)

    es = ExitStack()
    s_x = es.enter_context(nc.sbuf_tensor("s_x", [128, KT * TOKH], BF16))
    s_w = es.enter_context(nc.sbuf_tensor("s_w", [128, 2 * KT * 128], BF16))
    s_wx = es.enter_context(nc.sbuf_tensor("s_wx", [128, XT * 33], BF16))
    s_xz = es.enter_context(nc.sbuf_tensor("s_xz", [128, TOKH], F32))
    s_cw = es.enter_context(nc.sbuf_tensor("s_cw", [128, XT * 4], F32))
    s_cb = es.enter_context(nc.sbuf_tensor("s_cb", [128, XT], F32))
    s_cv0 = es.enter_context(nc.sbuf_tensor("s_cv0", [128, TOK], F32))
    s_cv1 = es.enter_context(nc.sbuf_tensor("s_cv1", [128, TOK], F32))
    s_xc = es.enter_context(nc.sbuf_tensor("s_xc", [128, 2 * TOK], F32))
    s_xcb = es.enter_context(nc.sbuf_tensor("s_xcb", [128, 2 * TOK], BF16))
    s_zs = es.enter_context(nc.sbuf_tensor("s_zs", [128, 2 * TOKH], F32))
    s_p = es.enter_context(nc.sbuf_tensor("s_p", [64, TOK], F32))
    s_dtr = es.enter_context(nc.sbuf_tensor("s_dtr", [1, TOK], F32))
    s_sp1 = es.enter_context(nc.sbuf_tensor("s_sp1", [1, TOK], F32))
    s_sp2 = es.enter_context(nc.sbuf_tensor("s_sp2", [1, TOK], F32))
    s_dtb = es.enter_context(nc.sbuf_tensor("s_dtb", [128, DI // 128], F32))
    s_one1 = es.enter_context(nc.sbuf_tensor("s_one1", [128, 1], F32))
    s_cs = es.enter_context(nc.sbuf_tensor("s_cs", [128, 1], F32))
    s_mdtb = es.enter_context(nc.sbuf_tensor("s_mdtb", [1, 1], F32))
    s_idf = es.enter_context(nc.sbuf_tensor("s_idf", [128, 128], F32))
    ps0 = es.enter_context(nc.psum_tensor("ps0", [128, 512], F32))
    ps1 = es.enter_context(nc.psum_tensor("ps1", [128, 512], F32))
    ps_p0 = es.enter_context(nc.psum_tensor("ps_p0", [64, 512], F32))
    ps_p1 = es.enter_context(nc.psum_tensor("ps_p1", [64, 512], F32))
    ps_p2 = es.enter_context(nc.psum_tensor("ps_p2", [64, 512], F32))
    ps_p3 = es.enter_context(nc.psum_tensor("ps_p3", [64, 512], F32))
    ps_m = es.enter_context(nc.psum_tensor("ps_m", [1, 8], F32))
    ld = es.enter_context(nc.semaphore("ld"))
    wv = es.enter_context(nc.semaphore("wv"))
    mm = es.enter_context(nc.semaphore("mm"))
    ev = es.enter_context(nc.semaphore("ev"))
    cv = es.enter_context(nc.semaphore("cv"))
    sx = es.enter_context(nc.semaphore("sx"))
    cbs = es.enter_context(nc.semaphore("cbs"))
    pb = es.enter_context(nc.semaphore("pb"))
    md = es.enter_context(nc.semaphore("md"))
    pv = es.enter_context(nc.semaphore("pv"))
    pq = es.enter_context(nc.semaphore("pq"))
    pa = es.enter_context(nc.semaphore("pa"))
    sp = es.enter_context(nc.semaphore("sp"))
    stx = es.enter_context(nc.semaphore("stx"))
    stz = es.enter_context(nc.semaphore("stz"))
    st = es.enter_context(nc.semaphore("st"))
    blk = es.enter_context(nc.Block())
    if True:
        ps_p = [ps_p0, ps_p1, ps_p2, ps_p3]
        LD_FIXED = 16 * (KT + XT + 1 + 1 + 2 * XT)

        @blk.gpsimd
        def _(g):
            for k in range(KT):
                g.dma_start(out=s_x.ap()[:, ts(k, TOKH)],
                            in_=xf[128 * k:128 * (k + 1), :]).then_inc(ld, 16)
            for k in range(XT):
                g.dma_start(out=s_wx.ap()[:, 33 * k:33 * (k + 1)],
                            in_=w_x_t[128 * k:128 * (k + 1), :]).then_inc(ld, 16)
            for mi in range(MT):
                if mi >= 2:
                    g.wait_ge(mm, NCHN * (mi - 1))
                buf = mi % 2
                g.dma_start(
                    out=s_w.ap()[:, buf * KT * 128:(buf + 1) * KT * 128]
                        .rearrange("p (k c) -> p k c", k=KT),
                    in_=w_in_t[:, 128 * mi:128 * (mi + 1)]
                        .rearrange("(k p) c -> p k c", p=128),
                ).then_inc(wv, 16)

        @blk.sync
        def _(s):
            s.dma_start(out=s_idf[:], in_=idn).then_inc(ld, 16)
            s.dma_start(out=s_dtb[:], in_=dt_bias).then_inc(ld, 16)
            for j in range(XT):
                s.dma_start(out=s_cw.ap()[:, 4 * j:4 * (j + 1)],
                            in_=conv_w[128 * j:128 * (j + 1), :]).then_inc(ld, 16)
                s.dma_start(out=s_cb.ap()[:, j:j + 1],
                            in_=conv_b[128 * j:128 * (j + 1), :]).then_inc(ld, 16)
            for xi in range(XT):
                s.wait_ge(sx, xi + 1)
                s.dma_start(out=xc_f[128 * xi:128 * (xi + 1), :],
                            in_=s_xc.ap()[:, (xi % 2) * TOK:(xi % 2) * TOK + TOK]
                            ).then_inc(stx, 16)
            for zi in range(XT):
                s.wait_ge(ev, XT * NCHN + NCHN * (zi + 1))
                s.dma_start(out=zs_f[128 * zi:128 * (zi + 1), :],
                            in_=s_zs.ap()[:, (zi % 2) * TOKH + 3:(zi % 2) * TOKH + TOKH]
                            ).then_inc(stz, 16)
            s.wait_ge(pv, 1)
            s.dma_start(out=s_dtr[:], in_=s_p.ap()[32:33, :]).then_inc(pq, 16)
            s.wait_ge(pv, 2)
            s.dma_start(out=s_p.ap()[32:33, :], in_=s_dtr[:]).then_inc(pq, 16)
            s.wait_ge(pq, 32)
            s.dma_start(out=dtbc_t, in_=s_p.ap()[0:33, :]).then_inc(st, 16)
            s.wait_ge(st, 16)
            s.wait_ge(stx, 16 * XT)
            s.wait_ge(stz, 16 * XT)

        @blk.tensor
        def _(t):
            t.wait_ge(ld, LD_FIXED)
            t.wait_ge(md, 1)
            t.matmul(ps_m.ap()[0:1, 0:1], s_one1[:], s_cs[:],
                     start=True, stop=True).then_inc(md, 1)
            g = 0
            for mi in range(MT):
                t.wait_ge(wv, 16 * (mi + 1))
                buf = mi % 2
                for ci, (c0, cw) in enumerate(NCH):
                    psb = [ps0, ps1][g % 2]
                    if g >= 2:
                        t.wait_ge(ev, g - 1)
                    ins = None
                    for k in range(KT):
                        ins = t.matmul(
                            psb.ap()[:, 0:cw],
                            s_w.ap()[:, buf * KT * 128 + 128 * k:
                                     buf * KT * 128 + 128 * (k + 1)],
                            s_x.ap()[:, k * TOKH + c0:k * TOKH + c0 + cw],
                            start=(k == 0), stop=(k == KT - 1),
                        )
                    ins.then_inc(mm, 1)
                    g += 1
                if mi < XT:
                    t.wait_ge(cbs, mi + 1)
                    ins3 = None
                    for n in range(4):
                        ins3 = t.matmul(
                            ps_p[n].ap()[0:33, :],
                            s_wx.ap()[:, 33 * mi:33 * (mi + 1)],
                            s_xcb.ap()[:, (mi % 2) * TOK + 512 * n:
                                       (mi % 2) * TOK + 512 * (n + 1)],
                            start=(mi == 0), stop=(mi == XT - 1),
                        )
                    ins3.then_inc(pb, 1)

        @blk.vector
        def _(v):
            v.wait_ge(ld, LD_FIXED)
            v.memset(s_one1[:], 1.0)
            v.tensor_reduce(out=s_cs[:], in_=s_dtb[:], axis=mybir.AxisListType.X,
                            op=ALU.add).then_inc(md, 1)
            v.wait_ge(md, 2)
            v.tensor_scalar(out=s_mdtb[:], in0=ps_m.ap()[0:1, 0:1], scalar1=1.0 / DI,
                            scalar2=None, op0=ALU.mult).then_inc(md, 1)
            g = 0
            for mi in range(MT):
                if mi < XT:
                    xi = mi
                    for ci, (c0, cw) in enumerate(NCH):
                        v.wait_ge(mm, g + 1)
                        psb = [ps0, ps1][g % 2]
                        v.tensor_copy(out=s_xz.ap()[:, c0:c0 + cw],
                                      in_=psb.ap()[:, 0:cw]).then_inc(ev, 1)
                        g += 1
                    v.tensor_scalar(out=s_cv0[:], in0=s_xz.ap()[:, 0:TOK],
                                    scalar1=s_cw.ap()[:, 4 * xi:4 * xi + 1],
                                    scalar2=None, op0=ALU.mult)
                    v.scalar_tensor_tensor(out=s_cv1[:], in0=s_xz.ap()[:, 1:1 + TOK],
                                           scalar=s_cw.ap()[:, 4 * xi + 1:4 * xi + 2],
                                           in1=s_cv0[:], op0=ALU.mult, op1=ALU.add)
                    v.scalar_tensor_tensor(out=s_cv0[:], in0=s_xz.ap()[:, 2:2 + TOK],
                                           scalar=s_cw.ap()[:, 4 * xi + 2:4 * xi + 3],
                                           in1=s_cv1[:], op0=ALU.mult, op1=ALU.add)
                    v.scalar_tensor_tensor(out=s_cv1[:], in0=s_xz.ap()[:, 3:3 + TOK],
                                           scalar=s_cw.ap()[:, 4 * xi + 3:4 * xi + 4],
                                           in1=s_cv0[:], op0=ALU.mult,
                                           op1=ALU.add).then_inc(cv, 1)
                    v.wait_ge(sx, xi + 1)
                    if xi >= 2:
                        v.wait_ge(pb, xi - 1)
                    v.tensor_copy(out=s_xcb.ap()[:, (xi % 2) * TOK:(xi % 2) * TOK + TOK],
                                  in_=s_xc.ap()[:, (xi % 2) * TOK:(xi % 2) * TOK + TOK]
                                  ).then_inc(cbs, 1)
                else:
                    g += NCHN
            v.wait_ge(pb, XT)
            for n in range(4):
                ins = v.tensor_copy(out=s_p.ap()[0:33, 512 * n:512 * (n + 1)],
                                    in_=ps_p[n].ap()[0:33, :])
            ins.then_inc(pv, 1)
            v.wait_ge(pq, 16)
            v.wait_ge(md, 3)
            v.tensor_scalar(out=s_sp1[:], in0=s_dtr[:], scalar1=s_mdtb.ap()[0:1, 0:1],
                            scalar2=None, op0=ALU.add).then_inc(sp, 1)
            v.wait_ge(pa, 1)
            v.scalar_tensor_tensor(out=s_dtr[:], in0=s_sp1[:], scalar=0.0,
                                   in1=s_sp2[:], op0=ALU.max,
                                   op1=ALU.add).then_inc(pv, 1)

        @blk.scalar
        def _(sc):
            sc.wait_ge(ld, LD_FIXED)
            g = XT * NCHN
            for mi in range(MT):
                if mi < XT:
                    xi = mi
                    sc.wait_ge(cv, xi + 1)
                    if xi >= 2:
                        sc.wait_ge(stx, 16 * (xi - 1))
                    sc.activation(out=s_xc.ap()[:, (xi % 2) * TOK:(xi % 2) * TOK + TOK],
                                  in_=s_cv1[:], func=AF.Silu,
                                  bias=s_cb.ap()[:, xi:xi + 1]).then_inc(sx, 1)
                else:
                    zi = mi - XT
                    if zi >= 2:
                        sc.wait_ge(stz, 16 * (zi - 1))
                    ins = None
                    for ci, (c0, cw) in enumerate(NCH):
                        sc.wait_ge(mm, g + 1)
                        psb = [ps0, ps1][g % 2]
                        ins = sc.activation(
                            out=s_zs.ap()[:, (zi % 2) * TOKH + c0:
                                          (zi % 2) * TOKH + c0 + cw],
                            in_=psb.ap()[:, 0:cw], func=AF.Silu)
                        ins.then_inc(ev, 1)
                        g += 1
            sc.wait_ge(sp, 1)
            sc.activation(out=s_sp2[:], in_=s_sp1[:], func=AF.Abs)
            sc.activation(out=s_sp2[:], in_=s_sp2[:], func=AF.Exp, scale=-1.0)
            sc.activation(out=s_sp2[:], in_=s_sp2[:], func=AF.Ln,
                          bias=1.0).then_inc(pa, 1)

    return nc


def a1_in_maps(L0_attn_x, L0_attn_W_in, L0_attn_conv_w, L0_attn_conv_b,
               L0_attn_W_x, L0_attn_dt_bias):
    xt = _f32c(L0_attn_x).reshape(B * S, DM)
    w_in_t = _f32c(np.asarray(L0_attn_W_in).T)
    conv_w = _f32c(np.asarray(L0_attn_conv_w).reshape(DI, 4))
    conv_b = _f32c(np.asarray(L0_attn_conv_b).reshape(DI, 1))
    w_x_t = _f32c(np.asarray(L0_attn_W_x).T)
    dt_bias = _f32c(np.asarray(L0_attn_dt_bias).reshape(128, DI // 128))
    idn = np.eye(128, dtype=np.float32)
    in_maps = []
    for c in range(NCORE):
        t0 = c * TOK
        xh = np.zeros((TOKH, DM), np.float32)
        xh[3:] = xt[t0:t0 + TOK]
        nprev = min(3, t0 % S)
        if nprev:
            xh[3 - nprev:3] = xt[t0 - nprev:t0]
        in_maps.append(dict(
            xf=_f32c(xh.T), w_in_t=w_in_t, conv_w=conv_w, conv_b=conv_b,
            w_x_t=w_x_t, dt_bias=dt_bias, idn=idn,
        ))
    return in_maps


def kernel(**inputs):
    raise NotImplementedError("phase A in progress")




def build_a2(ch_tot=None):
    import os
    nc = new_nc()
    TC = 512
    NCH2 = S // TC
    CH_TOT = ch_tot if ch_tot is not None else int(os.environ.get("A2_CHT", B * NCH2))
    NT = 64
    xcs = nc.dram_tensor("xcs", [DSL, B * S], F32, kind="ExternalInput").ap()
    zss = nc.dram_tensor("zss", [DSL, B * S], F32, kind="ExternalInput").ap()
    dtbc_t = nc.dram_tensor("dtbc_t", [33, B * S], F32, kind="ExternalInput").ap()
    a_perm = nc.dram_tensor("a_perm", [128, 64], F32, kind="ExternalInput").ap()
    dvec = nc.dram_tensor("dvec", [128, 4], F32, kind="ExternalInput").ap()
    sel = nc.dram_tensor("sel", [33, 33 * 128], F32, kind="ExternalInput").ap()
    idn = nc.dram_tensor("idn", [128, 128], F32, kind="ExternalInput").ap()
    ygs = nc.dram_tensor("ygs", [DSL, B * S], F32, kind="ExternalOutput").ap()
    dbg_on = bool(int(os.environ.get("A2_DBG", "0")))
    if dbg_on:
        dbg = nc.dram_tensor("dbg", [128, 8 * 512], F32, kind="ExternalOutput").ap()

    es = ExitStack()
    s_p33 = es.enter_context(nc.sbuf_tensor("s_p33", [33, 2 * TC], F32))
    s_sel = es.enter_context(nc.sbuf_tensor("s_sel", [33, 33 * 128], F32))
    s_dt = es.enter_context(nc.sbuf_tensor("s_dt", [128, 2 * TC], F32))
    s_cr = es.enter_context(nc.sbuf_tensor("s_cr", [128, 2 * DS * TC], F32))
    s_vr = es.enter_context(nc.sbuf_tensor("s_vr", [128, DS * TC], F32))
    s_xc2 = es.enter_context(nc.sbuf_tensor("s_xc2", [128, 2 * 4 * TC], F32))
    s_zs2 = es.enter_context(nc.sbuf_tensor("s_zs2", [128, 2 * 4 * TC], F32))
    s_a = es.enter_context(nc.sbuf_tensor("s_a", [128, 2 * TC], F32))
    s_u = es.enter_context(nc.sbuf_tensor("s_u", [128, 2 * TC], F32))
    s_h = es.enter_context(nc.sbuf_tensor("s_h", [128, 2 * TC], F32))
    s_chh = es.enter_context(nc.sbuf_tensor("s_chh", [128, 2 * TC], F32))
    s_ini = es.enter_context(nc.sbuf_tensor("s_ini", [128, NT], F32))
    s_ascr = es.enter_context(nc.sbuf_tensor("s_ascr", [128, NT], F32))
    s_asc = es.enter_context(nc.sbuf_tensor("s_asc", [128, NT], F32))
    s_dv = es.enter_context(nc.sbuf_tensor("s_dv", [128, 4], F32))
    s_idf2 = es.enter_context(nc.sbuf_tensor("s_idf2", [128, 128], F32))
    s_y1 = es.enter_context(nc.sbuf_tensor("s_y1", [128, TC], F32))
    s_dbg = es.enter_context(nc.sbuf_tensor("s_dbg", [128, 8 * 512], F32))
    s_ini2 = es.enter_context(nc.sbuf_tensor("s_ini2", [128, 64], F32))
    s_yg = es.enter_context(nc.sbuf_tensor("s_yg", [128, TC], F32))
    ps_y0 = es.enter_context(nc.psum_tensor("ps_y0", [128, TC], F32))
    ps_y1 = es.enter_context(nc.psum_tensor("ps_y1", [128, TC], F32))
    ps_y2 = es.enter_context(nc.psum_tensor("ps_y2", [128, TC], F32))
    ps_y3 = es.enter_context(nc.psum_tensor("ps_y3", [128, TC], F32))
    ps_bc0 = es.enter_context(nc.psum_tensor("ps_bc0", [128, TC], F32))
    ps_bc1 = es.enter_context(nc.psum_tensor("ps_bc1", [128, TC], F32))
    ps_bb0 = es.enter_context(nc.psum_tensor("ps_bb0", [128, TC], F32))
    ps_bb1 = es.enter_context(nc.psum_tensor("ps_bb1", [128, TC], F32))
    ld2 = es.enter_context(nc.semaphore("ld2"))
    lc = es.enter_context(nc.semaphore("lc"))
    pr = es.enter_context(nc.semaphore("pr"))
    rp = es.enter_context(nc.semaphore("rp"))
    red = es.enter_context(nc.semaphore("red"))
    rec = es.enter_context(nc.semaphore("rec"))
    rev = es.enter_context(nc.semaphore("rev"))
    ymk = es.enter_context(nc.semaphore("ymk"))
    aa = es.enter_context(nc.semaphore("aa"))
    hh = es.enter_context(nc.semaphore("hh"))
    ic = es.enter_context(nc.semaphore("ic"))
    chp = es.enter_context(nc.semaphore("chp"))
    ye = es.enter_context(nc.semaphore("ye"))
    st2 = es.enter_context(nc.semaphore("st2"))
    blk = es.enter_context(nc.Block())
    if True:
        ps_y = [ps_y0, ps_y1, ps_y2, ps_y3]
        ps_bc = [ps_bc0, ps_bc1]
        ps_bb = [ps_bb0, ps_bb1]
        LC_PER = 16 * 9

        @blk.sync
        def _(s):
            s.dma_start(out=s_sel[:], in_=sel).then_inc(ld2, 16)
            s.dma_start(out=s_idf2[:], in_=idn).then_inc(ld2, 16)
            s.dma_start(out=s_dv[:], in_=dvec).then_inc(ld2, 16)
            s.dma_start(out=s_ascr[:], in_=a_perm).then_inc(ld2, 16)
            for cc in range(CH_TOT + 1):
                if cc < CH_TOT:
                    t0 = (cc // NCH2) * S + (cc % NCH2) * TC
                    buf = cc % 2
                    if cc >= 2:
                        s.wait_ge(rp, 33 * (cc - 1))
                        s.wait_ge(ye, 4 * (cc - 1))
                    s.dma_start(out=s_p33.ap()[:, buf * TC:(buf + 1) * TC],
                                in_=dtbc_t[:, t0:t0 + TC]).then_inc(lc, 16)
                    for j in range(4):
                        s.dma_start(out=s_xc2.ap()[:, (buf * 4 + j) * TC:
                                                   (buf * 4 + j + 1) * TC],
                                    in_=xcs[128 * j:128 * (j + 1), t0:t0 + TC]
                                    ).then_inc(lc, 16)
                        s.dma_start(out=s_zs2.ap()[:, (buf * 4 + j) * TC:
                                                   (buf * 4 + j + 1) * TC],
                                    in_=zss[128 * j:128 * (j + 1), t0:t0 + TC]
                                    ).then_inc(lc, 16)
                if cc >= 1:
                    cs = cc - 1
                    t0s = (cs // NCH2) * S + (cs % NCH2) * TC
                    for j in range(4):
                        s.wait_ge(ye, 4 * cs + j + 1)
                        s.dma_start(out=ygs[128 * j:128 * (j + 1), t0s:t0s + TC],
                                    in_=s_yg[:]).then_inc(st2, 16)
            if dbg_on:
                s.wait_ge(ic, 100)
                s.dma_start(out=dbg, in_=s_dbg[:]).then_inc(st2, 16)
                s.wait_ge(st2, 16 * (4 * CH_TOT + 1))
            else:
                s.wait_ge(st2, 16 * 4 * CH_TOT)

        @blk.tensor
        def _(t):
            t.wait_ge(ld2, 64)
            for cc in range(CH_TOT):
                buf = cc % 2
                t.wait_ge(lc, LC_PER * cc + 16)
                if cc >= 1:
                    t.wait_ge(rec, 16 * (cc - 1) + 15)
                t.matmul(ps_bc[0][:], s_sel.ap()[:, 128 * 32:128 * 33],
                         s_p33.ap()[:, buf * TC:(buf + 1) * TC],
                         start=True, stop=True).then_inc(rp, 1)
                for si in range(DS):
                    if 16 * cc + si >= 2:
                        t.wait_ge(rev, 16 * cc + si - 1)
                    t.matmul(ps_bb[si % 2][:], s_sel.ap()[:, 128 * si:128 * (si + 1)],
                             s_p33.ap()[:, buf * TC:(buf + 1) * TC],
                             start=True, stop=True).then_inc(rp, 1)
                for si in range(DS):
                    if si == 0:
                        t.wait_ge(red, cc + 1)
                    elif si == 1:
                        if cc >= 1:
                            t.wait_ge(rec, 16 * cc)
                    else:
                        t.wait_ge(rec, 16 * cc + si - 1)
                    t.matmul(ps_bc[si % 2][:],
                             s_sel.ap()[:, 128 * (16 + si):128 * (17 + si)],
                             s_p33.ap()[:, buf * TC:(buf + 1) * TC],
                             start=True, stop=True).then_inc(rp, 1)
                for k in range(NT):
                    si, j = k // 4, k % 4
                    tb = k % 2
                    t.wait_ge(chp, cc * NT + k + 1)
                    if cc >= 1 and si == 0:
                        t.wait_ge(ye, 4 * (cc - 1) + j + 1)
                    t.matmul(ps_y[j][:], s_idf2[:],
                             s_chh.ap()[:, tb * TC:(tb + 1) * TC],
                             start=(si == 0), stop=(si == DS - 1)).then_inc(ymk, 1)

        @blk.scalar
        def _(sc):
            sc.wait_ge(ld2, 64)
            sc.activation(out=s_ascr[:], in_=s_ascr[:], func=AF.Exp).then_inc(pr, 1)
            for cc in range(CH_TOT):
                buf = cc % 2
                sc.wait_ge(rp, 33 * cc + 1)
                if cc >= 2:
                    sc.wait_ge(rev, 16 * (cc - 1))
                sc.activation(out=s_dt.ap()[:, buf * TC:(buf + 1) * TC],
                              in_=ps_bc[0][:], func=AF.Identity).then_inc(red, 1)
                for si in range(DS):
                    sc.wait_ge(rp, 33 * cc + 18 + si)
                    if cc >= 2:
                        sc.wait_ge(chp, NT * (cc - 1))
                    sc.activation(out=s_cr.ap()[:, (buf * DS + si) * TC:
                                                (buf * DS + si + 1) * TC],
                                  in_=ps_bc[si % 2][:], func=AF.Identity).then_inc(rec, 1)
                sc.wait_ge(pr, 2)
                for k in range(NT):
                    tb = k % 2
                    if cc * NT + k >= 2:
                        sc.wait_ge(hh, cc * NT + k - 1)
                    sc.activation(out=s_a.ap()[:, tb * TC:(tb + 1) * TC],
                                  in_=s_dt.ap()[:, buf * TC:(buf + 1) * TC],
                                  func=AF.Exp,
                                  scale=s_asc.ap()[:, k:k + 1]).then_inc(aa, 1)

        @blk.vector
        def _(v):
            v.wait_ge(ld2, 64)
            v.wait_ge(pr, 1)
            v.tensor_scalar(out=s_asc[:], in0=s_ascr[:], scalar1=-1.0, scalar2=None,
                            op0=ALU.mult).then_inc(pr, 1)
            for cc in range(CH_TOT):
                buf = cc % 2
                for si in range(DS):
                    v.wait_ge(rp, 33 * cc + 2 + si)
                    v.wait_ge(red, cc + 1)
                    if cc >= 1 and si < 2:
                        v.wait_ge(hh, NT * cc)
                    v.scalar_tensor_tensor(
                        out=s_vr.ap()[:, si * TC:(si + 1) * TC],
                        in0=ps_bb[si % 2][:], scalar=1.0,
                        in1=s_dt.ap()[:, buf * TC:(buf + 1) * TC],
                        op0=ALU.mult, op1=ALU.mult).then_inc(rev, 1)
                if dbg_on and cc == int(os.environ.get('A2_DBG_CC', '0')):
                    v.tensor_copy(out=s_ini2[:], in_=s_ini[:])
                for k in range(NT):
                    si, j = k // 4, k % 4
                    tb = k % 2
                    if cc * NT + k >= 2:
                        v.wait_ge(hh, cc * NT + k - 1)
                    v.tensor_tensor(
                        out=s_u.ap()[:, tb * TC:(tb + 1) * TC],
                        in0=s_xc2.ap()[:, (buf * 4 + j) * TC:(buf * 4 + j + 1) * TC],
                        in1=s_vr.ap()[:, si * TC:(si + 1) * TC],
                        op=ALU.mult)
                    v.wait_ge(aa, cc * NT + k + 1)
                    if cc * NT + k >= 2:
                        v.wait_ge(chp, cc * NT + k - 1)
                    if k == 0:
                        v.wait_ge(ic, cc)
                    ini_arg = 0.0 if (cc % NCH2) == 0 else s_ini.ap()[:, k:k + 1]
                    v.tensor_tensor_scan(out=s_h.ap()[:, tb * TC:(tb + 1) * TC],
                                         data0=s_a.ap()[:, tb * TC:(tb + 1) * TC],
                                         data1=s_u.ap()[:, tb * TC:(tb + 1) * TC],
                                         initial=ini_arg,
                                         op0=ALU.mult, op1=ALU.add).then_inc(hh, 1)
                    v.wait_ge(hh, cc * NT + k + 1)
                    cpy = v.tensor_copy(out=s_ini.ap()[:, k:k + 1],
                                  in_=s_h.ap()[:, tb * TC + TC - 1:tb * TC + TC])
                    if k == NT - 1:
                        cpy.then_inc(ic, 1)
                    if dbg_on and cc == int(os.environ.get('A2_DBG_CC', '0')) and k == 0:
                        v.tensor_copy(out=s_dbg.ap()[:, 0:TC], in_=s_a.ap()[:, 0:TC])
                        v.tensor_copy(out=s_dbg.ap()[:, TC:2 * TC], in_=s_u.ap()[:, 0:TC])
                        v.tensor_copy(out=s_dbg.ap()[:, 2 * TC:3 * TC], in_=s_h.ap()[:, 0:TC])
                        v.tensor_copy(out=s_dbg.ap()[:, 3 * TC:4 * TC],
                                      in_=s_vr.ap()[:, 0:TC])
                        v.tensor_copy(out=s_dbg.ap()[:, 4 * TC:5 * TC],
                                      in_=s_dt.ap()[:, 0:TC])
                        v.tensor_copy(out=s_dbg.ap()[:, 5 * TC:6 * TC],
                                      in_=s_cr.ap()[:, 0:TC])
                        v.tensor_copy(out=s_dbg.ap()[:, 6 * TC:7 * TC],
                                      in_=s_xc2.ap()[:, (buf * 4 + 0) * TC:
                                                     (buf * 4 + 1) * TC])
                        v.tensor_copy(out=s_dbg.ap()[:, 7 * TC:7 * TC + NT],
                                      in_=s_ini2[:]).then_inc(ic, 100)
                for j in range(4):
                    v.wait_ge(ymk, cc * NT + 60 + j + 1)
                    v.wait_ge(st2, 16 * (4 * cc + j))
                    v.scalar_tensor_tensor(
                        out=s_y1[:],
                        in0=s_xc2.ap()[:, (buf * 4 + j) * TC:(buf * 4 + j + 1) * TC],
                        scalar=s_dv.ap()[:, j:j + 1],
                        in1=ps_y[j].ap()[:, :], op0=ALU.mult, op1=ALU.add)
                    v.tensor_tensor(out=s_yg[:], in0=s_y1[:],
                                    in1=s_zs2.ap()[:, (buf * 4 + j) * TC:
                                                   (buf * 4 + j + 1) * TC],
                                    op=ALU.mult).then_inc(ye, 1)

        @blk.gpsimd
        def _(g):
            for cc in range(CH_TOT):
                buf = cc % 2
                for k in range(NT):
                    si = k // 4
                    tb = k % 2
                    g.wait_ge(hh, cc * NT + k + 1)
                    g.wait_ge(rec, 16 * cc + si + 1)
                    if cc * NT + k >= 2:
                        g.wait_ge(ymk, cc * NT + k - 1)
                    g.tensor_tensor(
                        out=s_chh.ap()[:, tb * TC:(tb + 1) * TC],
                        in0=s_h.ap()[:, tb * TC:(tb + 1) * TC],
                        in1=s_cr.ap()[:, (buf * DS + si) * TC:(buf * DS + si + 1) * TC],
                        op=ALU.mult).then_inc(chp, 1)
    return nc

def a2_in_maps(xc_full, zs_full, dtbct_full, A_log, D):
    A_log = np.asarray(A_log)
    D = np.asarray(D)
    sel = np.zeros((33, 33 * 128), np.float32)
    for r in range(33):
        sel[r, r * 128:(r + 1) * 128] = 1.0
    idn = np.eye(128, dtype=np.float32)
    in_maps = []
    for c in range(NCORE):
        d0 = c * DSL
        a_perm = np.empty((128, 64), np.float32)
        dvec = np.empty((128, 4), np.float32)
        for k in range(64):
            a_perm[:, k] = A_log[d0 + (k % 4) * 128:(d0 + (k % 4) * 128) + 128, k // 4]
        for j in range(4):
            dvec[:, j] = D[d0 + j * 128:d0 + (j + 1) * 128]
        in_maps.append(dict(
            xcs=_f32c(xc_full[d0:d0 + DSL]), zss=_f32c(zs_full[d0:d0 + DSL]),
            dtbc_t=_f32c(dtbct_full), a_perm=a_perm, dvec=dvec, sel=sel, idn=idn,
        ))
    return in_maps


FF0 = 8192


class Emit:
    ENGS = ("tensor", "vector", "scalar", "gpsimd", "sync")

    def __init__(self, nc, es):
        self.nc = nc
        self.es = es
        self.q = {e: [] for e in self.ENGS}
        self.cnt = {}
        self.sems = {}

    def sem(self, name):
        if name not in self.sems:
            self.sems[name] = self.es.enter_context(self.nc.semaphore(name))
            self.cnt[name] = 0
        return self.sems[name]

    def on(self, eng, fn):
        self.q[eng].append(fn)

    def wait(self, eng, name, val):
        h = self.sem(name)
        self.on(eng, lambda e, h=h, v=val: e.wait_ge(h, v))

    def bump(self, name, k=1):
        self.sem(name)
        self.cnt[name] += k
        return self.cnt[name]

    def play(self, blk):
        for eng in self.ENGS:
            fns = self.q[eng]
            dec = getattr(blk, eng)

            def make(fns):
                def body(e):
                    for f in fns:
                        f(e)
                return body
            dec(make(fns))


def emit_lnproj_phase(E, tag, w_dram, x_dram, res_dram, out_dram, KT, tiles,
                      outb=None, cast_x=True, prev_e=None, ncol=512):
    t = tiles
    NCKl, MTl = TOK // ncol, 16
    for n in range(NCKl):
        for k in range(KT):
            if n > 0:
                E.wait("gpsimd", tag + "m", MTl * n)
            E.wait("gpsimd", tag + "x", E.cnt.get(tag + "x", 0))
            xv = E.bump(tag + "x", 16)
            E.on("gpsimd", lambda e, k=k, n=n: e.dma_start(
                out=t["s_xch"].ap()[:, ncol * k:ncol * (k + 1)],
                in_=x_dram[128 * k:128 * (k + 1), ncol * n:ncol * (n + 1)]
            ).then_inc(E.sems[tag + "x"], 16))
        for m in range(MTl):
            g = n * MTl + m
            if g >= 2:
                E.wait("gpsimd", tag + "m", g - 1)
            nseg = 2 if KT > 64 else 1
            kseg = KT // nseg
            for sgi in range(nseg):
                E.wait("gpsimd", tag + "w", E.cnt.get(tag + "w", 0))
                wv = E.bump(tag + "w", 16)
                E.on("gpsimd", lambda e, m=m, g=g, sgi=sgi, kseg=kseg: e.dma_start(
                    out=t["s_w"].ap()[:, (g % 2) * KT * 128 + sgi * kseg * 128:
                                      (g % 2) * KT * 128 + (sgi + 1) * kseg * 128]
                        .rearrange("p (k c) -> p k c", k=kseg),
                    in_=w_dram[sgi * kseg * 128:(sgi + 1) * kseg * 128,
                               128 * m:128 * (m + 1)]
                        .rearrange("(k p) c -> p k c", p=128),
                ).then_inc(E.sems[tag + "w"], 16))
            if g >= 2:
                E.wait("sync", tag + "e", g - 1)
            E.wait("sync", tag + "r", E.cnt.get(tag + "r", 0))
            rv = E.bump(tag + "r", 16)
            E.on("sync", lambda e, m=m, n=n, g=g: e.dma_start(
                out=t["s_xr"].ap()[:, (g % 2) * 512:(g % 2) * 512 + ncol],
                in_=res_dram[128 * m:128 * (m + 1), ncol * n:ncol * (n + 1)]
            ).then_inc(E.sems[tag + "r"], 16))
            E.wait("tensor", tag + "w", wv)
            E.wait("tensor", tag + "x", 16 * KT * (n + 1))
            if g >= 2:
                E.wait("tensor", tag + "e", g - 1)
            elif prev_e is not None:
                E.wait("tensor", prev_e[0], prev_e[1])
            mv = E.bump(tag + "m")
            E.on("tensor", lambda e, g=g: [
                e.matmul(t["ps"][g % 2].ap()[:, 0:ncol],
                         t["s_w"].ap()[:, (g % 2) * KT * 128 + 128 * k:
                                       (g % 2) * KT * 128 + 128 * (k + 1)],
                         t["s_xch"].ap()[:, ncol * k:ncol * (k + 1)],
                         start=(k == 0), stop=(k == KT - 1))
                for k in range(KT)][-1].then_inc(E.sems[tag + "m"], 1))
            E.wait("vector", tag + "m", mv)
            E.wait("vector", tag + "r", rv)
            ev = E.bump(tag + "e")
            E.on("vector", lambda e, m=m, g=g: e.tensor_tensor(
                out=t["s_s"].ap()[:, 512 * m:512 * m + ncol],
                in0=t["ps"][g % 2].ap()[:, 0:ncol],
                in1=t["s_xr"].ap()[:, (g % 2) * 512:(g % 2) * 512 + ncol],
                op=ALU.add).then_inc(E.sems[tag + "e"], 1))
        ecnt = E.cnt[tag + "e"]

        def presq(m, n=n, base=ecnt - MTl):
            return (tag + "e", base + m + 1)
        rev = emit_ln_apply2(E, tag, t, presq, ncol=ncol)
        for m in range(MTl):
            a_idx = n * MTl + m
            E.wait("vector", tag + "rre", rev)
            if a_idx >= 2:
                E.wait("vector", tag + "st", 16 * (a_idx - 1))
            E.on("vector", lambda e, m=m, a=a_idx: e.tensor_tensor(
                out=t["s_t1"].ap()[:, 0:ncol], in0=t["s_s"].ap()[:, 512 * m:512 * m + ncol],
                in1=t["s_mr"].ap()[:, 0:ncol], op=ALU.subtract))
            av = E.bump(tag + "a")
            E.on("vector", lambda e, a=a_idx: e.tensor_tensor(
                out=t["s_out"].ap()[:, (a % 2) * 512:(a % 2) * 512 + ncol],
                in0=t["s_t1"].ap()[:, 0:ncol], in1=t["s_rr"].ap()[:, 0:ncol],
                op=ALU.mult).then_inc(E.sems[tag + "a"], 1))
            if outb is not None:
                bv = E.bump(tag + "b")
                E.on("vector", lambda e, m=m, n=n, a=a_idx: e.tensor_copy(
                    out=outb.ap()[:, m * TOK + 512 * n:m * TOK + 512 * (n + 1)],
                    in_=t["s_out"].ap()[:, (a % 2) * 512:(a % 2) * 512 + ncol]
                ).then_inc(E.sems[tag + "b"], 1))
            E.wait("sync", tag + "a", av)
            E.wait("sync", tag + "st", E.cnt.get(tag + "st", 0))
            E.bump(tag + "st", 16)
            E.on("sync", lambda e, m=m, n=n, a=a_idx: e.dma_start(
                out=out_dram[128 * m:128 * (m + 1), ncol * n:ncol * (n + 1)],
                in_=t["s_out"].ap()[:, (a % 2) * 512:(a % 2) * 512 + ncol]
            ).then_inc(E.sems[tag + "st"], 16))


def emit_ln_apply2(E, tag, t, presq, eps=1e-5, dm=DM, ncol=512):
    MTl = 16
    re_prev = E.cnt.get(tag + "rre", 0)
    E.sem(tag + "rre")
    sm_base = E.cnt.get(tag + "sm", 0)
    E.sem(tag + "sm")
    for m in range(MTl):
        nm, vv = presq(m)
        E.wait("scalar", nm, vv)
        if m >= 2:
            E.wait("scalar", tag + "sm", sm_base + m - 1)
        sqv = E.bump(tag + "sq")
        E.on("scalar", lambda e, m=m: e.activation(
            out=t["s_sq"].ap()[:, (m % 2) * 512:(m % 2) * 512 + ncol],
            in_=t["s_s"].ap()[:, 512 * m:512 * m + ncol],
            func=AF.Square).then_inc(E.sems[tag + "sq"], 1))
        E.wait("tensor", tag + "sq", sqv)
        if m == 0 and re_prev > 0:
            E.wait("tensor", tag + "rre", re_prev)
        E.on("tensor", lambda e, m=m: e.matmul(
            t["ps_s1"].ap()[0:1, 0:ncol], t["s_on"].ap()[:, 0:1],
            t["s_s"].ap()[:, 512 * m:512 * m + ncol],
            start=(m == 0), stop=(m == MTl - 1)))
        smv = E.bump(tag + "sm")
        E.on("tensor", lambda e, m=m: e.matmul(
            t["ps_s2"].ap()[0:1, 0:ncol], t["s_on"].ap()[:, 0:1],
            t["s_sq"].ap()[:, (m % 2) * 512:(m % 2) * 512 + ncol],
            start=(m == 0), stop=(m == MTl - 1)).then_inc(E.sems[tag + "sm"], 1))
    E.wait("vector", tag + "sm", E.cnt[tag + "sm"])
    E.on("vector", lambda e: e.tensor_scalar(
        out=t["s_st"].ap()[0:1, 0:ncol], in0=t["ps_s1"].ap()[0:1, 0:ncol],
        scalar1=1.0 / dm, scalar2=None, op0=ALU.mult))
    E.on("vector", lambda e: e.tensor_tensor(
        out=t["s_st"].ap()[0:1, 1024:1024 + ncol], in0=t["s_st"].ap()[0:1, 0:ncol],
        in1=t["s_st"].ap()[0:1, 0:ncol], op=ALU.mult))
    scv = E.bump(tag + "sc")
    E.on("vector", lambda e: e.scalar_tensor_tensor(
        out=t["s_st"].ap()[0:1, 512:512 + ncol], in0=t["ps_s2"].ap()[0:1, 0:ncol],
        scalar=1.0 / dm, in1=t["s_st"].ap()[0:1, 1024:1024 + ncol], op0=ALU.mult,
        op1=ALU.subtract))
    E.on("vector", lambda e: e.tensor_scalar(
        out=t["s_st"].ap()[0:1, 512:512 + ncol], in0=t["s_st"].ap()[0:1, 512:512 + ncol],
        scalar1=float(eps), scalar2=None,
        op0=ALU.add).then_inc(E.sems[tag + "sc"], 1))
    E.wait("scalar", tag + "sc", scv)
    scv = E.bump(tag + "sc")
    E.on("scalar", lambda e: e.activation(
        out=t["s_st"].ap()[0:1, 512:512 + ncol], in_=t["s_st"].ap()[0:1, 512:512 + ncol],
        func=AF.Sqrt).then_inc(E.sems[tag + "sc"], 1))
    E.wait("vector", tag + "sc", scv)
    scv = E.bump(tag + "sc")
    E.on("vector", lambda e: e.reciprocal(
        out=t["s_st"].ap()[0:1, 512:512 + ncol], in_=t["s_st"].ap()[0:1, 512:512 + ncol]
    ).then_inc(E.sems[tag + "sc"], 1))
    E.wait("tensor", tag + "sc", scv)
    E.on("tensor", lambda e: e.matmul(
        t["ps_r"].ap()[:, 0:ncol], t["s_on"].ap()[0:1, 0:128],
        t["s_st"].ap()[0:1, 0:ncol], start=True, stop=True))
    rrv = E.bump(tag + "rr")
    E.on("tensor", lambda e: e.matmul(
        t["ps_r"].ap()[:, 512:512 + ncol], t["s_on"].ap()[0:1, 0:128],
        t["s_st"].ap()[0:1, 512:512 + ncol], start=True,
        stop=True).then_inc(E.sems[tag + "rr"], 1))
    E.wait("vector", tag + "rr", rrv)
    E.on("vector", lambda e: e.tensor_copy(out=t["s_mr"].ap()[:, 0:ncol], in_=t["ps_r"].ap()[:, 0:ncol]))
    rev = E.bump(tag + "rre")
    E.on("vector", lambda e: e.tensor_copy(
        out=t["s_rr"].ap()[:, 0:ncol], in_=t["ps_r"].ap()[:, 512:512 + ncol]
    ).then_inc(E.sems[tag + "rre"], 1))
    return rev

def build_a3():
    import os
    nc = new_nc()
    ygf = nc.dram_tensor("ygf", [DI, TOK], F32, kind="ExternalInput").ap()
    xres = nc.dram_tensor("xres", [DM, TOK], F32, kind="ExternalInput").ap()
    ow_t = nc.dram_tensor("ow_t", [DI, DM], F32, kind="ExternalInput").ap()
    wg_t = nc.dram_tensor("wg_t", [DM, FF0], F32, kind="ExternalInput").ap()
    wu_t = nc.dram_tensor("wu_t", [DM, FF0], F32, kind="ExternalInput").ap()
    wd_t = nc.dram_tensor("wd_t", [FF0, DM], F32, kind="ExternalInput").ap()
    ones = nc.dram_tensor("ones", [128, 128], F32, kind="ExternalInput").ap()
    r1f = nc.dram_tensor("r1f", [DM, TOK], F32, kind="ExternalOutput").ap()
    r2f = nc.dram_tensor("r2f", [DM, TOK], F32, kind="ExternalOutput").ap()
    h_sc = nc.dram_tensor("h_sc", [FF0, TOK], BF16, kind="Internal").ap()

    es = ExitStack()
    E = Emit(nc, es)
    t = {}
    s_big1 = es.enter_context(nc.sbuf_tensor("s_big1", [128, 64 * 512], BF16))
    s_big2 = es.enter_context(nc.sbuf_tensor("s_big2", [128, 32 * 512], BF16))
    t["s_xch"] = s_big2
    t["s_w"] = es.enter_context(nc.sbuf_tensor("s_w", [128, 2 * 64 * 128], BF16))
    t["s_xr"] = es.enter_context(nc.sbuf_tensor("s_xr", [128, 2 * 512], F32))
    t["s_s"] = es.enter_context(nc.sbuf_tensor("s_s", [128, 16 * 512], F32))
    t["s_sq"] = es.enter_context(nc.sbuf_tensor("s_sq", [128, 1024], F32))
    t["s_st"] = es.enter_context(nc.sbuf_tensor("s_st", [1, 1536], F32))
    t["s_mr"] = es.enter_context(nc.sbuf_tensor("s_mr", [128, 512], F32))
    t["s_rr"] = es.enter_context(nc.sbuf_tensor("s_rr", [128, 512], F32))
    t["s_on"] = es.enter_context(nc.sbuf_tensor("s_on", [128, 128], F32))
    t["s_out"] = es.enter_context(nc.sbuf_tensor("s_out", [128, 2 * 512], F32))
    t["s_t1"] = es.enter_context(nc.sbuf_tensor("s_t1", [128, 512], F32))
    s_r1b = s_big1
    s_g3 = es.enter_context(nc.sbuf_tensor("s_g3", [128, 512], F32))
    s_hb = es.enter_context(nc.sbuf_tensor("s_hb", [128, 2 * 512], BF16))
    ps0 = es.enter_context(nc.psum_tensor("ps0", [128, 512], F32))
    ps1 = es.enter_context(nc.psum_tensor("ps1", [128, 512], F32))
    ps2 = es.enter_context(nc.psum_tensor("ps2", [128, 512], F32))
    ps3 = es.enter_context(nc.psum_tensor("ps3", [128, 512], F32))
    t["ps"] = [ps0, ps1]
    t["ps_s1"] = es.enter_context(nc.psum_tensor("ps_s1", [1, 512], F32))
    t["ps_s2"] = es.enter_context(nc.psum_tensor("ps_s2", [1, 512], F32))
    t["ps_r"] = es.enter_context(nc.psum_tensor("ps_r", [128, 1024], F32))
    blk_holder = []

    lv = E.bump("ld", 16)
    E.on("sync", lambda e: e.dma_start(out=t["s_on"][:], in_=ones
                                       ).then_inc(E.sems["ld"], 16))
    for eng in ("tensor", "vector", "scalar", "gpsimd"):
        E.wait(eng, "ld", lv)

    emit_lnproj_phase(E, "p1", ow_t, ygf, xres, r1f, KT=32, tiles=t, outb=s_r1b)

    phases = os.environ.get("A3_PH", "123")
    psg = [ps0, ps1]
    psu = [ps2, ps3]
    p2n = int(os.environ.get("A3_P2N", FF0 // 128)) if "2" in phases else 0
    only_loads = bool(os.environ.get("A3_P2ONLYLOADS"))
    if p2n:
        E.wait("gpsimd", "p1m", E.cnt.get("p1m", 0))
    for hm in range(p2n):
        for wi, wd_ in ((0, wg_t), (1, wu_t)):
            gidx = hm * 2 + wi
            if gidx >= 4 and not only_loads:
                E.wait("gpsimd", "p2m", 4 * ((gidx - 4) // 2 + 1))
            E.wait("gpsimd", "p2w", E.cnt.get("p2w", 0))
            wv = E.bump("p2w", 16)
            E.on("gpsimd", lambda e, hm=hm, wd_=wd_, gidx=gidx: e.dma_start(
                out=t["s_w"].ap()[:, (gidx % 4) * 2048:(gidx % 4) * 2048 + 2048]
                    .rearrange("p (k c) -> p k c", k=16),
                in_=wd_[:, 128 * hm:128 * (hm + 1)]
                    .rearrange("(k p) c -> p k c", p=128),
            ).then_inc(E.sems["p2w"], 16))
        for n in range(4 if not only_loads else 0):
            gi = hm * 4 + n
            E.wait("tensor", "p2w", 16 * (hm * 2 + 2))
            E.wait("tensor", "p1b", 16 * (n + 1))
            if gi >= 2:
                E.wait("tensor", "p2e", gi - 1)
            else:
                E.wait("tensor", "p1e", 64)
            mv = E.bump("p2m")
            if os.environ.get("A3_NOGATE"):
                E.on("tensor", lambda e, gi=gi: e.matmul(
                    psg[gi % 2][:], t["s_w"].ap()[:, 0:128],
                    s_r1b.ap()[:, 0:512], start=True, stop=True))
            else:
                E.on("tensor", lambda e, hm=hm, n=n, gi=gi: [
                    e.matmul(psg[gi % 2][:],
                             t["s_w"].ap()[:, ((hm * 2) % 4) * 2048 + 128 * k:
                                       ((hm * 2) % 4) * 2048 + 128 * (k + 1)],
                             s_r1b.ap()[:, k * TOK + 512 * n:k * TOK + 512 * (n + 1)],
                             start=(k == 0), stop=(k == 15)) for k in range(16)][-1])
            if os.environ.get("A3_NOUP"):
                E.on("tensor", lambda e, gi=gi: e.matmul(
                    psu[gi % 2][:], t["s_w"].ap()[:, 0:128],
                    s_r1b.ap()[:, 0:512], start=True,
                    stop=True).then_inc(E.sems["p2m"], 1))
            else:
                E.on("tensor", lambda e, hm=hm, n=n, gi=gi: [
                    e.matmul(psu[gi % 2][:],
                             t["s_w"].ap()[:, ((hm * 2 + 1) % 4) * 2048 + 128 * k:
                                       ((hm * 2 + 1) % 4) * 2048 + 128 * (k + 1)],
                             s_r1b.ap()[:, k * TOK + 512 * n:k * TOK + 512 * (n + 1)],
                             start=(k == 0), stop=(k == 15)) for k in range(16)
                ][-1].then_inc(E.sems["p2m"], 1))
            if os.environ.get("A3_SILU_SBUF"):
                E.wait("vector", "p2m", mv)
                sgv0 = E.bump("p2s0")
                E.on("vector", lambda e, gi=gi: e.tensor_copy(
                    out=s_g3[:], in_=psg[gi % 2].ap()[:, :]
                ).then_inc(E.sems["p2s0"], 1))
                E.wait("scalar", "p2s0", sgv0)
                sgv = E.bump("p2s")
                E.on("scalar", lambda e: e.activation(
                    out=s_g3[:], in_=s_g3[:], func=AF.Silu
                ).then_inc(E.sems["p2s"], 1))
            else:
                E.wait("scalar", "p2m", mv)
                sgv = E.bump("p2s")
                E.on("scalar", lambda e, gi=gi: e.activation(
                    out=s_g3[:], in_=psg[gi % 2][:], func=AF.Silu
                ).then_inc(E.sems["p2s"], 1))
            E.wait("vector", "p2s", sgv)
            if gi >= 2:
                E.wait("vector", "p2st", 16 * (gi - 1))
            ev = E.bump("p2e")
            E.on("vector", lambda e, gi=gi: e.tensor_tensor(
                out=s_hb.ap()[:, (gi % 2) * 512:(gi % 2) * 512 + 512],
                in0=s_g3[:], in1=psu[gi % 2].ap()[:, :],
                op=ALU.mult).then_inc(E.sems["p2e"], 1))
            E.wait("sync", "p2e", ev)
            E.wait("sync", "p2st", E.cnt.get("p2st", 0))
            E.bump("p2st", 16)
            if os.environ.get("A3_STFIX"):
                E.on("sync", lambda e, gi=gi: e.dma_start(
                    out=h_sc[0:128, 0:512],
                    in_=s_hb.ap()[:, (gi % 2) * 512:(gi % 2) * 512 + 512]
                ).then_inc(E.sems["p2st"], 16))
            else:
                E.on("sync", lambda e, hm=hm, n=n, gi=gi: e.dma_start(
                    out=h_sc[128 * hm:128 * (hm + 1), 512 * n:512 * (n + 1)],
                    in_=s_hb.ap()[:, (gi % 2) * 512:(gi % 2) * 512 + 512]
                ).then_inc(E.sems["p2st"], 16))

    if only_loads:
        E.wait("gpsimd", "p2w", E.cnt.get("p2w", 0))
    if "3" in phases:
        E.wait("gpsimd", "p2st", E.cnt["p2st"])
        t3 = dict(t)
        t3["s_xch"] = s_big1
        emit_lnproj_phase(E, "p3", wd_t, h_sc, r1f, r2f, KT=64, tiles=t3,
                          cast_x=False,
                          prev_e=("p2e", E.cnt.get("p2e", 0)) if "2" in phases
                          else ("p1e", 64))
        E.wait("sync", "p3st", E.cnt["p3st"])
    E.wait("sync", "p1st", E.cnt["p1st"])

    blk = es.enter_context(nc.Block())
    E.play(blk)
    return nc

def a3_in_maps(yg_full, x, out_W, w_gate, w_up, w_down):
    xt = _f32c(x).reshape(B * S, DM)
    ow_t = _f32c(np.asarray(out_W).T)
    wg_t = _f32c(np.asarray(w_gate).T)
    wu_t = _f32c(np.asarray(w_up).T)
    wd_t = _f32c(np.asarray(w_down).T)
    ones = np.ones((128, 128), np.float32)
    in_maps = []
    for c in range(NCORE):
        t0 = c * TOK
        in_maps.append(dict(
            ygf=_f32c(yg_full[:, t0:t0 + TOK]),
            xres=_f32c(xt[t0:t0 + TOK].T),
            ow_t=ow_t, wg_t=wg_t, wu_t=wu_t, wd_t=wd_t, ones=ones,
        ))
    return in_maps

QKVD = 3072

def build_a4():
    nc = new_nc()
    r2h = nc.dram_tensor("r2h", [DM, TOKH], F32, kind="ExternalInput").ap()
    qkv_t = nc.dram_tensor("qkv_t", [DM, QKVD], F32, kind="ExternalInput").ap()
    bgk_t = nc.dram_tensor("bgk_t", [DM, 32], F32, kind="ExternalInput").ap()
    cv1 = nc.dram_tensor("cv1", [QKVD, 4], F32, kind="ExternalInput").ap()
    ones = nc.dram_tensor("ones", [128, 128], F32, kind="ExternalInput").ap()
    qn = nc.dram_tensor("qn", [2048, TOK], F32, kind="ExternalOutput").ap()
    kn = nc.dram_tensor("kn", [512, TOK], F32, kind="ExternalOutput").ap()
    vv = nc.dram_tensor("vv", [512, TOK], F32, kind="ExternalOutput").ap()
    beta_g = nc.dram_tensor("beta_g", [32, TOK], F32, kind="ExternalOutput").ap()

    es = ExitStack()
    E = Emit(nc, es)
    s_x = es.enter_context(nc.sbuf_tensor("s_x", [128, 16 * TOKH], BF16))
    s_w = es.enter_context(nc.sbuf_tensor("s_w", [128, 2 * 16 * 128], BF16))
    s_cw = es.enter_context(nc.sbuf_tensor("s_cw", [128, 24 * 4], F32))
    s_qz = es.enter_context(nc.sbuf_tensor("s_qz", [128, TOKH], F32))
    s_c0 = es.enter_context(nc.sbuf_tensor("s_c0", [128, TOK], F32))
    s_c1 = es.enter_context(nc.sbuf_tensor("s_c1", [128, TOK], F32))
    s_qc = es.enter_context(nc.sbuf_tensor("s_qc", [128, 2 * TOK], F32))
    s_sq = es.enter_context(nc.sbuf_tensor("s_sq", [128, 512], F32))
    s_rs = es.enter_context(nc.sbuf_tensor("s_rs", [128, 512], F32))
    s_on = es.enter_context(nc.sbuf_tensor("s_on", [128, 128], F32))
    s_o1 = es.enter_context(nc.sbuf_tensor("s_o1", [128, 2 * TOK], F32))
    s_bg = es.enter_context(nc.sbuf_tensor("s_bg", [32, TOK], F32))
    s_sg = es.enter_context(nc.sbuf_tensor("s_sg", [32, TOK], F32))
    s_t1 = es.enter_context(nc.sbuf_tensor("s_t4", [32, TOK], F32))
    ps0 = es.enter_context(nc.psum_tensor("ps0", [128, 512], F32))
    ps1 = es.enter_context(nc.psum_tensor("ps1", [128, 512], F32))
    ps_ss = es.enter_context(nc.psum_tensor("ps_ss", [128, 512], F32))
    ps_bg = es.enter_context(nc.psum_tensor("ps_bg", [32, 512], F32))
    NCH = [(0, 512), (512, 512), (1024, 512), (1536, 512), (2048, 3)]
    MT4 = QKVD // 128

    lv = E.bump("ld", 16)
    E.on("sync", lambda e: e.dma_start(out=s_on[:], in_=ones).then_inc(E.sems["ld"], 16))
    for j in range(MT4):
        E.bump("ld", 16)
        E.on("sync", lambda e, j=j: e.dma_start(
            out=s_cw.ap()[:, 4 * j:4 * (j + 1)],
            in_=cv1[128 * j:128 * (j + 1), :]).then_inc(E.sems["ld"], 16))
    for k in range(16):
        E.bump("ldx", 16)
        E.on("gpsimd", lambda e, k=k: e.dma_start(
            out=s_x.ap()[:, ts(k, TOKH)],
            in_=r2h[128 * k:128 * (k + 1), :]).then_inc(E.sems["ldx"], 16))
    for eng in ("tensor", "vector", "scalar"):
        E.wait(eng, "ld", E.cnt["ld"])
        E.wait(eng, "ldx", E.cnt["ldx"])

    for mi in range(MT4):
        if mi >= 2:
            E.wait("gpsimd", "mm", 5 * (mi - 1))
        E.wait("gpsimd", "wv", E.cnt.get("wv", 0))
        wv = E.bump("wv", 16)
        E.on("gpsimd", lambda e, mi=mi: e.dma_start(
            out=s_w.ap()[:, (mi % 2) * 2048:(mi % 2) * 2048 + 2048]
                .rearrange("p (k c) -> p k c", k=16),
            in_=qkv_t[:, 128 * mi:128 * (mi + 1)]
                .rearrange("(k p) c -> p k c", p=128)).then_inc(E.sems["wv"], 16))
        E.wait("tensor", "wv", wv)
        for ci, (c0, cw) in enumerate(NCH):
            g = mi * 5 + ci
            if g >= 2:
                E.wait("tensor", "ev", g - 1)
            mv = E.bump("mm")
            E.on("tensor", lambda e, mi=mi, c0=c0, cw=cw, g=g: [
                e.matmul(
                    [ps0, ps1][g % 2].ap()[:, 0:cw],
                    s_w.ap()[:, (mi % 2) * 2048 + 128 * k:(mi % 2) * 2048 + 128 * (k + 1)],
                    s_x.ap()[:, k * TOKH + c0:k * TOKH + c0 + cw],
                    start=(k == 0), stop=(k == 15)) for k in range(16)
            ][-1].then_inc(E.sems["mm"], 1))
            E.wait("vector", "mm", mv)
            ev = E.bump("ev")
            E.on("vector", lambda e, c0=c0, cw=cw, g=g: e.tensor_copy(
                out=s_qz.ap()[:, c0:c0 + cw],
                in_=[ps0, ps1][g % 2].ap()[:, 0:cw]).then_inc(E.sems["ev"], 1))
        cvv = E.bump("cv")
        E.on("vector", lambda e, mi=mi: (
            e.tensor_scalar(out=s_c0[:], in0=s_qz.ap()[:, 0:TOK],
                            scalar1=s_cw.ap()[:, 4 * mi:4 * mi + 1], scalar2=None,
                            op0=ALU.mult),
            e.scalar_tensor_tensor(out=s_c1[:], in0=s_qz.ap()[:, 1:1 + TOK],
                                   scalar=s_cw.ap()[:, 4 * mi + 1:4 * mi + 2],
                                   in1=s_c0[:], op0=ALU.mult, op1=ALU.add),
            e.scalar_tensor_tensor(out=s_c0[:], in0=s_qz.ap()[:, 2:2 + TOK],
                                   scalar=s_cw.ap()[:, 4 * mi + 2:4 * mi + 3],
                                   in1=s_c1[:], op0=ALU.mult, op1=ALU.add),
            e.scalar_tensor_tensor(out=s_c1[:], in0=s_qz.ap()[:, 3:3 + TOK],
                                   scalar=s_cw.ap()[:, 4 * mi + 3:4 * mi + 4],
                                   in1=s_c0[:], op0=ALU.mult,
                                   op1=ALU.add).then_inc(E.sems["cv"], 1))[-1])
        E.wait("scalar", "cv", cvv)
        if 2 <= mi < 22:
            E.wait("scalar", "nq", 4 * (mi - 1))
        if mi >= 22:
            E.wait("scalar", "stv", 16 * (mi - 21))
        sxv = E.bump("sx")
        E.on("scalar", lambda e, mi=mi: e.activation(
            out=s_qc.ap()[:, (mi % 2) * TOK:(mi % 2) * TOK + TOK], in_=s_c1[:],
            func=AF.Silu).then_inc(E.sems["sx"], 1))
        if mi < 20:
            for n in range(4):
                E.wait("scalar", "sx", sxv)
                sqv = E.bump("sq")
                E.on("scalar", lambda e, mi=mi, n=n: e.activation(
                    out=s_sq[:], in_=s_qc.ap()[:, (mi % 2) * TOK + 512 * n:
                                               (mi % 2) * TOK + 512 * (n + 1)],
                    func=AF.Square).then_inc(E.sems["sq"], 1))
                E.wait("tensor", "sq", sqv)
                if E.cnt.get("nq", 0) >= 1:
                    E.wait("tensor", "rsd", E.cnt.get("rsd", 0))
                ssv = E.bump("ssm")
                E.on("tensor", lambda e: e.matmul(
                    ps_ss[:], s_on[:], s_sq[:], start=True,
                    stop=True).then_inc(E.sems["ssm"], 1))
                E.wait("vector", "ssm", ssv)
                E.on("vector", lambda e: e.tensor_scalar(
                    out=s_rs[:], in0=ps_ss.ap()[:, :], scalar1=1e-6, scalar2=None,
                    op0=ALU.add))
                rv = E.bump("rs")
                E.on("vector", lambda e: e.nop().then_inc(E.sems["rs"], 1))
                E.wait("scalar", "rs", rv)
                rv2 = E.bump("rs")
                E.on("scalar", lambda e: e.activation(
                    out=s_rs[:], in_=s_rs[:], func=AF.Sqrt).then_inc(E.sems["rs"], 1))
                E.wait("vector", "rs", rv2)
                rsdv = E.bump("rsd")
                E.on("vector", lambda e: e.reciprocal(
                    out=s_rs[:], in_=s_rs[:]).then_inc(E.sems["rsd"], 1))
                if E.cnt.get("nq", 0) >= 2:
                    E.wait("vector", "stq", 16 * (E.cnt.get("nq", 0) - 1))
                nqv = E.bump("nq")
                cmul = (128.0 ** -0.5) if mi < 16 else 1.0
                E.on("vector", lambda e, mi=mi, n=n, cmul=cmul, nqv=nqv: e.scalar_tensor_tensor(
                    out=s_o1.ap()[:, (nqv % 2) * TOK:(nqv % 2) * TOK + 512],
                    in0=s_qc.ap()[:, (mi % 2) * TOK + 512 * n:
                                  (mi % 2) * TOK + 512 * (n + 1)],
                    scalar=cmul, in1=s_rs[:], op0=ALU.mult,
                    op1=ALU.mult).then_inc(E.sems["nq"], 1))
                E.wait("sync", "nq", nqv)
                E.bump("stq", 16)
                dst = qn if mi < 16 else kn
                row = 128 * mi if mi < 16 else 128 * (mi - 16)
                E.on("sync", lambda e, dst=dst, row=row, n=n, nqv=nqv: e.dma_start(
                    out=dst[row:row + 128, 512 * n:512 * (n + 1)],
                    in_=s_o1.ap()[:, (nqv % 2) * TOK:(nqv % 2) * TOK + 512]
                ).then_inc(E.sems["stq"], 16))
        else:
            E.wait("sync", "sx", sxv)
            E.bump("stv", 16)
            E.on("sync", lambda e, mi=mi: e.dma_start(
                out=vv[128 * (mi - 20):128 * (mi - 19), :],
                in_=s_qc.ap()[:, (mi % 2) * TOK:(mi % 2) * TOK + TOK]
            ).then_inc(E.sems["stv"], 16))
    E.wait("gpsimd", "mm", E.cnt["mm"])
    E.wait("gpsimd", "wv", E.cnt["wv"])
    wv = E.bump("wv", 16)
    E.on("gpsimd", lambda e: e.dma_start(
        out=s_w.ap()[:, 0:32].rearrange("p (k c) -> p k c", k=1) if False else
        s_w.ap()[:, 0:512].rearrange("p (k c) -> p k c", k=16),
        in_=bgk_t.rearrange("(k p) c -> p k c", p=128)).then_inc(E.sems["wv"], 16))
    E.wait("tensor", "wv", wv)
    for n in range(4):
        E.wait("tensor", "bge", E.cnt.get("bge", 0))
        bmv = E.bump("bgm")
        E.on("tensor", lambda e, n=n: [
            e.matmul(ps_bg.ap()[0:32, :],
                     s_w.ap()[:, 32 * k:32 * (k + 1)],
                     s_x.ap()[:, k * TOKH + 3 + 512 * n:k * TOKH + 3 + 512 * (n + 1)],
                     start=(k == 0), stop=(k == 15)) for k in range(16)
        ][-1].then_inc(E.sems["bgm"], 1))
        E.wait("scalar", "bgm", bmv)
        E.on("scalar", lambda e, n=n: e.activation(
            out=s_sg.ap()[0:32, 512 * n:512 * (n + 1)], in_=ps_bg.ap()[0:32, :],
            func=AF.Sigmoid))
        E.on("scalar", lambda e, n=n: e.activation(
            out=s_bg.ap()[0:32, 512 * n:512 * (n + 1)], in_=ps_bg.ap()[0:32, :],
            func=AF.Abs))
        E.on("scalar", lambda e, n=n: e.activation(
            out=s_bg.ap()[0:32, 512 * n:512 * (n + 1)],
            in_=s_bg.ap()[0:32, 512 * n:512 * (n + 1)], func=AF.Exp, scale=-1.0))
        sgv = E.bump("bgs")
        E.on("scalar", lambda e, n=n: e.activation(
            out=s_bg.ap()[0:32, 512 * n:512 * (n + 1)],
            in_=s_bg.ap()[0:32, 512 * n:512 * (n + 1)], func=AF.Ln,
            bias=1.0).then_inc(E.sems["bgs"], 1))
        E.wait("vector", "bgs", sgv)
        E.on("vector", lambda e, n=n: e.scalar_tensor_tensor(
            out=s_t1.ap()[0:32, 512 * n:512 * (n + 1)], in0=ps_bg.ap()[0:32, :],
            scalar=0.0, in1=s_bg.ap()[0:32, 512 * n:512 * (n + 1)],
            op0=ALU.min, op1=ALU.subtract))
        bev = E.bump("bge")
        E.on("vector", lambda e, n=n: e.tensor_scalar(
            out=s_t1.ap()[0:32, 512 * n:512 * (n + 1)],
            in0=s_t1.ap()[0:32, 512 * n:512 * (n + 1)], scalar1=1.0 / 16.0,
            scalar2=None, op0=ALU.mult).then_inc(E.sems["bge"], 1))
    E.wait("sync", "bge", E.cnt["bge"])
    E.bump("stb", 32)
    E.on("sync", lambda e: e.dma_start(out=beta_g[0:16, :], in_=s_sg.ap()[0:16, :]
                                       ).then_inc(E.sems["stb"], 16))
    E.on("sync", lambda e: e.dma_start(out=beta_g[16:32, :], in_=s_t1.ap()[16:32, :]
                                       ).then_inc(E.sems["stb"], 16))
    E.wait("sync", "stb", 32)
    E.wait("sync", "stq", E.cnt["stq"])
    E.wait("sync", "stv", E.cnt["stv"])

    blk = es.enter_context(nc.Block())
    E.play(blk)
    return nc


def build_a5a():
    import os
    nc = new_nc()
    CH = 128
    NC5 = S // CH
    NBH = int(os.environ.get("A5_NBH", "4"))
    NCH5 = int(os.environ.get("A5_NCH", str(NC5)))
    qf4 = nc.dram_tensor("qf4", [4 * 128, S], F32, kind="ExternalInput").ap()
    kf4 = nc.dram_tensor("kf4", [4 * 128, S], F32, kind="ExternalInput").ap()
    vf4 = nc.dram_tensor("vf4", [4 * 128, S], F32, kind="ExternalInput").ap()
    bg4 = nc.dram_tensor("bg4", [8, S], F32, kind="ExternalInput").ap()
    tri = nc.dram_tensor("tri", [128, 128], F32, kind="ExternalInput").ap()
    mus = nc.dram_tensor("mus", [128, 128], F32, kind="ExternalInput").ap()
    mls = nc.dram_tensor("mls", [128, 128], F32, kind="ExternalInput").ap()
    mui = nc.dram_tensor("mui", [128, 128], F32, kind="ExternalInput").ap()
    idf = nc.dram_tensor("idf", [128, 128], F32, kind="ExternalInput").ap()
    o4 = nc.dram_tensor("o4", [4 * S, 128], F32, kind="ExternalOutput").ap()

    es = ExitStack()
    E = Emit(nc, es)
    T = {}
    for nm, w, dt in [("kb", 128, BF16), ("qb", 128, BF16), ("vb", 128, BF16),
                      ("cc", 1, F32), ("bcol", 1, F32), ("gcol", 1, F32),
                      ("brow", 128, F32), ("crow", 128, F32),
                      ("crep", 128, F32), ("cl", 1, F32), ("brep", 128, F32),
                      ("dm", 128, F32), ("dmt", 128, F32), ("lrep", 128, F32),
                      ("llc", 1, F32), ("wlc", 1, F32), ("lc", 1, F32),
                      ("t1", 128, F32), ("t2", 128, F32),
                      ("N", 128, F32), ("Nb", 128, BF16),
                      ("M", 128, F32), ("Mb", 128, BF16),
                      ("ast", 128, F32), ("astb", 128, BF16),
                      ("W", 128, F32), ("Wb", 128, BF16),
                      ("WT", 128, F32), ("WTb", 128, BF16),
                      ("Qj", 128, F32), ("Qjb", 128, BF16),
                      ("QjT", 128, F32), ("QjTb", 128, BF16),
                      ("vt", 128, F32), ("kt", 128, F32),
                      ("rhs", 128, F32), ("rhsb", 128, BF16),
                      ("dl", 128, F32), ("dlb", 128, BF16),
                      ("qlb", 128, BF16), ("kk", 128, F32),
                      ("S", 128, F32), ("Sb", 128, BF16),
                      ("o", 128, F32),
                      ("tri", 128, F32), ("mus", 128, F32), ("mls", 128, F32),
                      ("mui", 128, F32), ("idf", 128, F32), ("idb", 128, BF16),
                      ("on1", 128, F32)]:
        T[nm] = es.enter_context(nc.sbuf_tensor("z_" + nm, [128, w], dt))
    pa = es.enter_context(nc.psum_tensor("pa", [128, 128], F32))
    pb = es.enter_context(nc.psum_tensor("pb", [128, 128], F32))
    pc = es.enter_context(nc.psum_tensor("pc", [128, 128], F32))
    pS = es.enter_context(nc.psum_tensor("pS", [128, 128], F32))
    po = es.enter_context(nc.psum_tensor("po", [128, 128], F32))
    pt = es.enter_context(nc.psum_tensor("pt", [128, 128], BF16))
    pw = es.enter_context(nc.psum_tensor("pw", [128, 128], F32))

    def tmm(fn):
        E.wait("tensor", "vq", E.cnt.get("vq", 0))
        E.wait("tensor", "sq", E.cnt.get("sq", 0))
        E.wait("tensor", "gq", E.cnt.get("gq", 0))
        tv = E.bump("tq")
        E.on("tensor", lambda e, fn=fn: fn(e).then_inc(E.sems["tq"], 1))
        return tv

    def vop(fn):
        E.wait("vector", "vq", E.cnt.get("vq", 0))
        E.wait("vector", "tq", E.cnt.get("tq", 0))
        E.wait("vector", "sq", E.cnt.get("sq", 0))
        E.wait("vector", "gq", E.cnt.get("gq", 0))
        vv = E.bump("vq")
        E.on("vector", lambda e, fn=fn: fn(e).then_inc(E.sems["vq"], 1))
        return vv

    def aop(fn):
        E.wait("scalar", "sq", E.cnt.get("sq", 0))
        E.wait("scalar", "tq", E.cnt.get("tq", 0))
        E.wait("scalar", "vq", E.cnt.get("vq", 0))
        sv = E.bump("sq")
        E.on("scalar", lambda e, fn=fn: fn(e).then_inc(E.sems["sq"], 1))
        return sv

    def gload(fn):
        E.wait("gpsimd", "vq", E.cnt.get("vq", 0))
        E.wait("gpsimd", "tq", E.cnt.get("tq", 0))
        gv = E.bump("gq", 16)
        E.on("gpsimd", lambda e, fn=fn: fn(e).then_inc(E.sems["gq"], 16))
        return gv

    for nm, src in [("tri", tri), ("mus", mus), ("mls", mls), ("mui", mui),
                    ("idf", idf)]:
        E.bump("gq", 16)
        E.on("gpsimd", lambda e, nm=nm, src=src: e.dma_start(
            out=T[nm][:], in_=src).then_inc(E.sems["gq"], 16))
    E.bump("gq", 16)
    E.on("gpsimd", lambda e: e.dma_start(out=T["idb"][:], in_=idf
                                         ).then_inc(E.sems["gq"], 16))
    E.wait("vector", "gq", E.cnt["gq"])
    vop(lambda e: e.memset(T["on1"][:], 1.0))

    for bh in range(NBH):
        vop(lambda e: e.memset(T["S"][:], 0.0))
        vop(lambda e: e.memset(T["Sb"][:], 0.0))
        for ci in range(NCH5):
            t0 = ci * CH
            r0 = bh * 128
            gload(lambda e, r0=r0, t0=t0: e.dma_start(
                out=T["kb"][:], in_=kf4[r0:r0 + 128, t0:t0 + CH]))
            gload(lambda e, r0=r0, t0=t0: e.dma_start(
                out=T["qb"][:], in_=qf4[r0:r0 + 128, t0:t0 + CH]))
            gload(lambda e, r0=r0, t0=t0: e.dma_start(
                out=T["vb"][:], in_=vf4[r0:r0 + 128, t0:t0 + CH]))
            gload(lambda e, bh=bh, t0=t0: e.dma_start(
                out=T["gcol"][:], in_=bg4[2 * bh + 1:2 * bh + 2, t0:t0 + CH]
                .transpose([1, 0])))
            gload(lambda e, bh=bh, t0=t0: e.dma_start(
                out=T["bcol"][:], in_=bg4[2 * bh:2 * bh + 1, t0:t0 + CH]
                .transpose([1, 0])))
            gload(lambda e, bh=bh, t0=t0: e.dma_start(
                out=T["brow"][0:1, :], in_=bg4[2 * bh:2 * bh + 1, t0:t0 + CH]))
            tmm(lambda e: e.matmul(pa.ap()[:, 0:1], T["tri"][:], T["gcol"][:],
                                   start=True, stop=True))
            vop(lambda e: e.tensor_copy(out=T["cc"][:], in_=pa.ap()[:, 0:1]))
            tmm(lambda e: e.transpose(pw.ap()[0:1, 0:128], T["cc"][:],
                                      T["idf"][:]))
            vop(lambda e: e.tensor_copy(out=T["crow"][0:1, :],
                                        in_=pw.ap()[0:1, 0:128]))
            tmm(lambda e: e.matmul(pa[:], T["on1"].ap()[0:1, 0:128],
                                   T["crow"].ap()[0:1, :], start=True, stop=True))
            tmm(lambda e: e.matmul(pb.ap()[:, 0:1], T["on1"].ap()[0:1, 0:128],
                                   T["crow"].ap()[0:1, 127:128], start=True,
                                   stop=True))
            tmm(lambda e: e.matmul(pc[:], T["on1"].ap()[0:1, 0:128],
                                   T["brow"].ap()[0:1, :], start=True, stop=True))
            vop(lambda e: e.tensor_copy(out=T["crep"][:], in_=pa[:]))
            vop(lambda e: e.tensor_copy(out=T["cl"][:], in_=pb.ap()[:, 0:1]))
            vop(lambda e: e.tensor_copy(out=T["brep"][:], in_=pc[:]))
            vop(lambda e: e.tensor_scalar(out=T["t1"][:], in0=T["crep"][:],
                                          scalar1=T["cc"][:], scalar2=0.0,
                                          op0=ALU.subtract, op1=ALU.max))
            aop(lambda e: e.activation(out=T["dm"][:], in_=T["t1"][:],
                                       func=AF.Exp, scale=-1.0))
            vop(lambda e: e.tensor_scalar(out=T["t2"][:], in0=T["crep"][:],
                                          scalar1=T["cc"][:], scalar2=0.0,
                                          op0=ALU.subtract, op1=ALU.min))
            aop(lambda e: e.activation(out=T["dmt"][:], in_=T["t2"][:],
                                       func=AF.Exp))
            aop(lambda e: e.activation(out=T["lrep"][:], in_=T["crep"][:],
                                       func=AF.Exp))
            aop(lambda e: e.activation(out=T["llc"][:], in_=T["cl"][:],
                                       func=AF.Exp))
            aop(lambda e: e.activation(out=T["wlc"][:], in_=T["cc"][:],
                                       func=AF.Exp, scale=-1.0, bias=T["cl"][:]))
            aop(lambda e: e.activation(out=T["lc"][:], in_=T["cc"][:],
                                       func=AF.Exp))
            tmm(lambda e: e.matmul(pa[:], T["kb"][:], T["kb"][:], start=True,
                                   stop=True))
            tmm(lambda e: e.matmul(pc[:], T["kb"][:], T["qb"][:], start=True,
                                   stop=True))
            vop(lambda e: e.tensor_tensor(out=T["t1"][:], in0=pa[:],
                                          in1=T["dmt"][:], op=ALU.mult))
            vop(lambda e: e.tensor_tensor(out=T["t2"][:], in0=T["t1"][:],
                                          in1=T["mus"][:], op=ALU.mult))
            vop(lambda e: e.tensor_tensor(out=T["N"][:], in0=T["t2"][:],
                                          in1=T["brep"][:], op=ALU.mult))
            vop(lambda e: e.tensor_copy(out=T["Nb"][:], in_=T["N"][:]))
            vop(lambda e: e.tensor_tensor(out=T["t1"][:], in0=pa[:],
                                          in1=T["dm"][:], op=ALU.mult))
            vop(lambda e: e.tensor_tensor(out=T["t2"][:], in0=T["t1"][:],
                                          in1=T["mls"][:], op=ALU.mult))
            vop(lambda e: e.tensor_scalar(out=T["M"][:], in0=T["t2"][:],
                                          scalar1=T["bcol"][:], scalar2=None,
                                          op0=ALU.mult))
            vop(lambda e: e.tensor_copy(out=T["Mb"][:], in_=T["M"][:]))
            vop(lambda e: e.tensor_tensor(out=T["t1"][:], in0=pc[:],
                                          in1=T["dmt"][:], op=ALU.mult))
            vop(lambda e: e.tensor_tensor(out=T["ast"][:], in0=T["t1"][:],
                                          in1=T["mui"][:], op=ALU.mult))
            vop(lambda e: e.tensor_copy(out=T["astb"][:], in_=T["ast"][:]))
            vop(lambda e: e.tensor_tensor(out=T["W"][:], in0=T["idf"][:],
                                          in1=T["N"][:], op=ALU.subtract))
            vop(lambda e: e.tensor_copy(out=T["Wb"][:], in_=T["W"][:]))
            vop(lambda e: e.tensor_tensor(out=T["WT"][:], in0=T["idf"][:],
                                          in1=T["M"][:], op=ALU.subtract))
            vop(lambda e: e.tensor_copy(out=T["WTb"][:], in_=T["WT"][:]))
            tmm(lambda e: e.matmul(pa[:], T["Mb"][:], T["Nb"][:], start=True,
                                   stop=True))
            tmm(lambda e: e.matmul(pb[:], T["Nb"][:], T["Mb"][:], start=True,
                                   stop=True))
            vop(lambda e: e.tensor_copy(out=T["Qj"][:], in_=pa[:]))
            vop(lambda e: e.tensor_copy(out=T["Qjb"][:], in_=T["Qj"][:]))
            vop(lambda e: e.tensor_copy(out=T["QjT"][:], in_=pb[:]))
            vop(lambda e: e.tensor_copy(out=T["QjTb"][:], in_=T["QjT"][:]))
            for lev in range(1, 7):
                tmm(lambda e: e.matmul(pa[:], T["WTb"][:], T["Qjb"][:],
                                       start=True, stop=True))
                tmm(lambda e: e.matmul(pb[:], T["Qjb"][:], T["WTb"][:],
                                       start=True, stop=True))
                vop(lambda e: e.tensor_tensor(out=T["W"][:], in0=T["W"][:],
                                              in1=pa[:], op=ALU.add))
                vop(lambda e: e.tensor_copy(out=T["Wb"][:], in_=T["W"][:]))
                vop(lambda e: e.tensor_tensor(out=T["WT"][:], in0=T["WT"][:],
                                              in1=pb[:], op=ALU.add))
                vop(lambda e: e.tensor_copy(out=T["WTb"][:], in_=T["WT"][:]))
                if lev < 6:
                    tmm(lambda e: e.matmul(pa[:], T["QjTb"][:], T["Qjb"][:],
                                           start=True, stop=True))
                    tmm(lambda e: e.matmul(pb[:], T["Qjb"][:], T["QjTb"][:],
                                           start=True, stop=True))
                    vop(lambda e: e.tensor_copy(out=T["Qj"][:], in_=pa[:]))
                    vop(lambda e: e.tensor_copy(out=T["Qjb"][:], in_=T["Qj"][:]))
                    vop(lambda e: e.tensor_copy(out=T["QjT"][:], in_=pb[:]))
                    vop(lambda e: e.tensor_copy(out=T["QjTb"][:], in_=T["QjT"][:]))
            tmm(lambda e: e.transpose(pt[:], T["vb"][:], T["idb"][:]))
            vop(lambda e: e.tensor_copy(out=T["vt"][:], in_=pt[:]))
            tmm(lambda e: e.transpose(pt[:], T["kb"][:], T["idb"][:]))
            vop(lambda e: e.tensor_copy(out=T["kt"][:], in_=pt[:]))
            tmm(lambda e: e.matmul(pa[:], T["kb"][:], T["Sb"][:], start=True,
                                   stop=True))
            vop(lambda e: e.tensor_scalar(out=T["t1"][:], in0=pa[:],
                                          scalar1=T["lc"][:], scalar2=None,
                                          op0=ALU.mult))
            vop(lambda e: e.tensor_tensor(out=T["t2"][:], in0=T["vt"][:],
                                          in1=T["t1"][:], op=ALU.subtract))
            vop(lambda e: e.tensor_scalar(out=T["rhs"][:], in0=T["t2"][:],
                                          scalar1=T["bcol"][:], scalar2=None,
                                          op0=ALU.mult))
            vop(lambda e: e.tensor_copy(out=T["rhsb"][:], in_=T["rhs"][:]))
            tmm(lambda e: e.matmul(pa[:], T["Wb"][:], T["rhsb"][:], start=True,
                                   stop=True))
            vop(lambda e: e.tensor_copy(out=T["dl"][:], in_=pa[:]))
            vop(lambda e: e.tensor_copy(out=T["dlb"][:], in_=T["dl"][:]))
            vop(lambda e: e.tensor_tensor(out=T["qlb"][:], in0=T["qb"][:],
                                          in1=T["lrep"][:], op=ALU.mult))
            tmm(lambda e: e.matmul(po[:], T["qlb"][:], T["Sb"][:], start=True,
                                   stop=False))
            tmm(lambda e: e.matmul(po[:], T["astb"][:], T["dlb"][:], start=False,
                                   stop=True))
            E.wait("vector", "st5", E.cnt.get("st5", 0))
            vop(lambda e: e.tensor_copy(out=T["o"][:], in_=po[:]))
            stv = E.bump("st5", 16)
            E.wait("sync", "vq", E.cnt["vq"])
            E.on("sync", lambda e, bh=bh, t0=t0: e.dma_start(
                out=o4[bh * S + t0:bh * S + t0 + CH, :],
                in_=T["o"][:]).then_inc(E.sems["st5"], 16))
            vop(lambda e: e.tensor_scalar(out=T["kk"][:], in0=T["kt"][:],
                                          scalar1=T["wlc"][:], scalar2=None,
                                          op0=ALU.mult))
            tmm(lambda e: e.matmul(pS[:], T["kk"][:], T["dl"][:], start=True,
                                   stop=True))
            vop(lambda e: e.scalar_tensor_tensor(out=T["S"][:], in0=T["S"][:],
                                                 scalar=T["llc"][:], in1=pS[:],
                                                 op0=ALU.mult, op1=ALU.add))
            vop(lambda e: e.tensor_copy(out=T["Sb"][:], in_=T["S"][:]))
    E.wait("sync", "st5", E.cnt["st5"])
    blk = es.enter_context(nc.Block())
    E.play(blk)
    return nc


def build_a5b():
    import os
    ph5 = os.environ.get("A5B_PH", "12")
    nc = new_nc()
    o_f = nc.dram_tensor("o_f", [2048, TOK], F32, kind="ExternalInput").ap()
    r2f_in = nc.dram_tensor("r2f_in", [DM, TOK], F32, kind="ExternalInput").ap()
    g_w_t = nc.dram_tensor("g_w_t", [DM, 2048], F32, kind="ExternalInput").ap()
    o_w_t = nc.dram_tensor("o_w_t", [2048, DM], F32, kind="ExternalInput").ap()
    onw = nc.dram_tensor("onw", [128, 1], F32, kind="ExternalInput").ap()
    ones = nc.dram_tensor("ones", [128, 128], F32, kind="ExternalInput").ap()
    r3f = nc.dram_tensor("r3f", [DM, TOK], F32, kind="ExternalOutput").ap()
    og_sc = nc.dram_tensor("og_sc", [2048, TOK], BF16, kind="Internal").ap()

    es = ExitStack()
    E = Emit(nc, es)
    t = {}
    t["s_xch"] = es.enter_context(nc.sbuf_tensor("s_xch", [128, 16 * 512], BF16))
    t["s_w"] = es.enter_context(nc.sbuf_tensor("s_w", [128, 2 * 16 * 128], BF16))
    t["s_xr"] = es.enter_context(nc.sbuf_tensor("s_xr", [128, 2 * 512], F32))
    t["s_s"] = es.enter_context(nc.sbuf_tensor("s_s", [128, 16 * 512], F32))
    t["s_sq"] = es.enter_context(nc.sbuf_tensor("s_sq", [128, 1024], F32))
    t["s_st"] = es.enter_context(nc.sbuf_tensor("s_st", [1, 1536], F32))
    t["s_mr"] = es.enter_context(nc.sbuf_tensor("s_mr", [128, 512], F32))
    t["s_rr"] = es.enter_context(nc.sbuf_tensor("s_rr", [128, 512], F32))
    t["s_on"] = es.enter_context(nc.sbuf_tensor("s_on", [128, 128], F32))
    t["s_out"] = es.enter_context(nc.sbuf_tensor("s_out", [128, 2 * 512], F32))
    t["s_t1"] = es.enter_context(nc.sbuf_tensor("s_t1", [128, 512], F32))
    s_ot = es.enter_context(nc.sbuf_tensor("s_ot", [128, 2 * 512], F32))
    s_rs5 = es.enter_context(nc.sbuf_tensor("s_rs5", [128, 512], F32))
    s_sg5 = es.enter_context(nc.sbuf_tensor("s_sg5", [128, 512], F32))
    s_onw = es.enter_context(nc.sbuf_tensor("s_onw", [128, 1], F32))
    s_og = es.enter_context(nc.sbuf_tensor("s_og", [128, 2 * 512], BF16))
    ps0 = es.enter_context(nc.psum_tensor("ps0", [128, 512], F32))
    ps1 = es.enter_context(nc.psum_tensor("ps1", [128, 512], F32))
    ps_g = es.enter_context(nc.psum_tensor("ps_g", [128, 512], F32))
    ps_ss5 = es.enter_context(nc.psum_tensor("ps_ss5", [128, 512], F32))
    t["ps"] = [ps0, ps1]
    t["ps_s1"] = es.enter_context(nc.psum_tensor("ps_s1", [1, 512], F32))
    t["ps_s2"] = es.enter_context(nc.psum_tensor("ps_s2", [1, 512], F32))
    t["ps_r"] = es.enter_context(nc.psum_tensor("ps_r", [128, 1024], F32))

    lv = E.bump("ld", 32)
    E.on("sync", lambda e: e.dma_start(out=t["s_on"][:], in_=ones
                                       ).then_inc(E.sems["ld"], 16))
    E.on("sync", lambda e: e.dma_start(out=s_onw[:], in_=onw
                                       ).then_inc(E.sems["ld"], 16))
    for eng in ("tensor", "vector", "scalar", "gpsimd"):
        E.wait(eng, "ld", lv)
    for n in range(4 if "1" in ph5 else 0):
        for k in range(16):
            E.wait("gpsimd", "g1m", 16 * n)
            E.wait("gpsimd", "g1x", E.cnt.get("g1x", 0))
            E.bump("g1x", 16)
            E.on("gpsimd", lambda e, k=k, n=n: e.dma_start(
                out=t["s_xch"].ap()[:, 512 * k:512 * (k + 1)],
                in_=r2f_in[128 * k:128 * (k + 1), 512 * n:512 * (n + 1)]
            ).then_inc(E.sems["g1x"], 16))
        for m in range(16):
            g = n * 16 + m
            if g >= 2:
                E.wait("gpsimd", "g1m", g - 1)
            E.wait("gpsimd", "g1w", E.cnt.get("g1w", 0))
            wv = E.bump("g1w", 16)
            E.on("gpsimd", lambda e, m=m, g=g: e.dma_start(
                out=t["s_w"].ap()[:, (g % 2) * 2048:(g % 2) * 2048 + 2048]
                    .rearrange("p (k c) -> p k c", k=16),
                in_=g_w_t[:, 128 * m:128 * (m + 1)]
                    .rearrange("(k p) c -> p k c", p=128)).then_inc(E.sems["g1w"], 16))
            if g >= 2:
                E.wait("sync", "g1e", g - 1)
            E.wait("sync", "g1o", E.cnt.get("g1o", 0))
            ov = E.bump("g1o", 16)
            E.on("sync", lambda e, m=m, n=n, g=g: e.dma_start(
                out=s_ot.ap()[:, (g % 2) * 512:(g % 2) * 512 + 512],
                in_=o_f[128 * m:128 * (m + 1), 512 * n:512 * (n + 1)]
            ).then_inc(E.sems["g1o"], 16))
            E.wait("tensor", "g1w", wv)
            E.wait("tensor", "g1x", 16 * 16 * (n + 1))
            if g >= 2:
                E.wait("tensor", "g1e", g - 1)
            if g >= 1:
                E.wait("tensor", "g1sg", g)
            mv = E.bump("g1m")
            E.on("tensor", lambda e, g=g: [
                e.matmul(ps_g[:],
                         t["s_w"].ap()[:, (g % 2) * 2048 + 128 * k:
                                       (g % 2) * 2048 + 128 * (k + 1)],
                         t["s_xch"].ap()[:, 512 * k:512 * (k + 1)],
                         start=(k == 0), stop=(k == 15)) for k in range(16)
            ][-1].then_inc(E.sems["g1m"], 1))
            E.wait("scalar", "g1o", ov)
            sqv = E.bump("g1sq")
            E.on("scalar", lambda e, g=g: e.activation(
                out=t["s_sq"].ap()[:, 0:512],
                in_=s_ot.ap()[:, (g % 2) * 512:(g % 2) * 512 + 512],
                func=AF.Square).then_inc(E.sems["g1sq"], 1))
            E.wait("tensor", "g1sq", sqv)
            if g >= 1:
                E.wait("tensor", "g1r", 2 * g)
            ssv = E.bump("g1ss")
            E.on("tensor", lambda e: e.matmul(
                ps_ss5[:], t["s_on"][:], t["s_sq"].ap()[:, 0:512],
                start=True, stop=True).then_inc(E.sems["g1ss"], 1))
            E.wait("vector", "g1ss", ssv)
            E.wait("vector", "g1m", mv)
            E.on("vector", lambda e: e.tensor_scalar(
                out=s_rs5[:], in0=ps_ss5.ap()[:, :], scalar1=1.0 / 128.0,
                scalar2=1e-5, op0=ALU.mult, op1=ALU.add))
            rv = E.bump("g1r")
            E.on("vector", lambda e: e.reciprocal(out=s_rs5[:], in_=s_rs5[:]
                                                  ).then_inc(E.sems["g1r"], 1))
            E.wait("scalar", "g1r", rv)
            rv2 = E.bump("g1r")
            E.on("scalar", lambda e: e.activation(
                out=s_rs5[:], in_=s_rs5[:], func=AF.Sqrt).then_inc(E.sems["g1r"], 1))
            sgv = E.bump("g1sg")
            E.on("scalar", lambda e: e.activation(
                out=s_sg5[:], in_=ps_g[:], func=AF.Silu).then_inc(E.sems["g1sg"], 1))
            E.wait("vector", "g1r", rv2)
            E.wait("vector", "g1sg", sgv)
            if g >= 2:
                E.wait("vector", "g1st", 16 * (g - 1))
            ev = E.bump("g1e")
            E.on("vector", lambda e, g=g: (
                e.scalar_tensor_tensor(
                    out=t["s_t1"][:],
                    in0=s_ot.ap()[:, (g % 2) * 512:(g % 2) * 512 + 512],
                    scalar=s_onw[:], in1=s_rs5[:], op0=ALU.mult, op1=ALU.mult),
                e.tensor_tensor(
                    out=s_og.ap()[:, (g % 2) * 512:(g % 2) * 512 + 512],
                    in0=t["s_t1"][:], in1=s_sg5[:],
                    op=ALU.mult).then_inc(E.sems["g1e"], 1))[-1])
            E.wait("sync", "g1e", ev)
            E.wait("sync", "g1st", E.cnt.get("g1st", 0))
            E.bump("g1st", 16)
            E.on("sync", lambda e, m=m, n=n, g=g: e.dma_start(
                out=og_sc[128 * m:128 * (m + 1), 512 * n:512 * (n + 1)],
                in_=s_og.ap()[:, (g % 2) * 512:(g % 2) * 512 + 512]
            ).then_inc(E.sems["g1st"], 16))
    if "2" in ph5:
        if "1" in ph5:
            E.wait("gpsimd", "g1st", E.cnt["g1st"])
        emit_lnproj_phase(E, "q5", o_w_t, og_sc, r2f_in, r3f, KT=16, tiles=t,
                          cast_x=False,
                          prev_e=("g1e", E.cnt["g1e"]) if "1" in ph5 else None)
        E.wait("sync", "q5st", E.cnt["q5st"])
    blk = es.enter_context(nc.Block())
    E.play(blk)
    return nc

I32 = mybir.dt.int32

def build_a6():
    nc = new_nc()
    gpk = nc.dram_tensor("gpk", [2048, 1024], I32, kind="ExternalInput").ap()
    gsc = nc.dram_tensor("gsc", [2048, 64], F32, kind="ExternalInput").ap()
    upk = nc.dram_tensor("upk", [2048, 1024], I32, kind="ExternalInput").ap()
    usc = nc.dram_tensor("usc", [2048, 64], F32, kind="ExternalInput").ap()
    dpk = nc.dram_tensor("dpk", [256, 8192], I32, kind="ExternalInput").ap()
    dsc = nc.dram_tensor("dsc", [256, 512], F32, kind="ExternalInput").ap()
    idf = nc.dram_tensor("idf", [128, 128], F32, kind="ExternalInput").ap()
    wgt_s = nc.dram_tensor("wgt_s", [2048, 2048], BF16, kind="ExternalOutput").ap()
    wut_s = nc.dram_tensor("wut_s", [2048, 2048], BF16, kind="ExternalOutput").ap()
    wdt_s = nc.dram_tensor("wdt_s", [16384, 256], BF16, kind="ExternalOutput").ap()

    es = ExitStack()
    E = Emit(nc, es)
    T = {}
    for nm, w, dt in [("pk", 1024, I32), ("sc", 64, F32), ("u", 1024, F32),
                      ("s8", 1024, F32), ("w", 1024, F32), ("t1", 1024, F32),
                      ("t2", 1024, F32), ("e", 1024, F32), ("f", 1024, F32),
                      ("p2", 1024, F32), ("m", 1024, F32), ("sg", 1024, F32),
                      ("dq", 2048, F32), ("dqb", 2048, BF16), ("tb", 128, BF16),
                      ("idb", 128, BF16)]:
        T[nm] = es.enter_context(nc.sbuf_tensor("w_" + nm, [128, w], dt))
    pt6 = es.enter_context(nc.psum_tensor("pt6", [128, 128], BF16))

    def tmm(fn):
        E.wait("tensor", "vq", E.cnt.get("vq", 0))
        E.wait("tensor", "sq", E.cnt.get("sq", 0))
        E.wait("tensor", "gq", E.cnt.get("gq", 0))
        tv = E.bump("tq")
        E.on("tensor", lambda e, fn=fn: fn(e).then_inc(E.sems["tq"], 1))
        return tv

    def vop(fn):
        E.wait("vector", "vq", E.cnt.get("vq", 0))
        E.wait("vector", "tq", E.cnt.get("tq", 0))
        E.wait("vector", "sq", E.cnt.get("sq", 0))
        E.wait("vector", "gq", E.cnt.get("gq", 0))
        E.wait("vector", "st6", E.cnt.get("st6", 0))
        vv = E.bump("vq")
        E.on("vector", lambda e, fn=fn: fn(e).then_inc(E.sems["vq"], 1))
        return vv

    def aop(fn):
        E.wait("scalar", "sq", E.cnt.get("sq", 0))
        E.wait("scalar", "vq", E.cnt.get("vq", 0))
        sv = E.bump("sq")
        E.on("scalar", lambda e, fn=fn: fn(e).then_inc(E.sems["sq"], 1))
        return sv

    def gload(fn):
        E.wait("gpsimd", "vq", E.cnt.get("vq", 0))
        E.wait("gpsimd", "gq", E.cnt.get("gq", 0))
        gv = E.bump("gq", 16)
        E.on("gpsimd", lambda e, fn=fn: fn(e).then_inc(E.sems["gq"], 16))
        return gv

    gload(lambda e: e.dma_start(out=T["idb"][:], in_=idf))
    LN2 = float(np.log(2.0))

    def dequant_unit(pk_ap, sc_ap, wcols, out_dram, out_row0, out_col0, trows):
        gload(lambda e: e.dma_start(out=T["pk"].ap()[:, 0:wcols], in_=pk_ap))
        gload(lambda e: e.dma_start(out=T["sc"].ap()[:, 0:wcols // 16], in_=sc_ap))
        for half in range(2):
            if half == 0:
                vop(lambda e: e.tensor_scalar(
                    out=T["u"].ap()[:, 0:wcols].bitcast(I32),
                    in0=T["pk"].ap()[:, 0:wcols], scalar1=15, scalar2=None,
                    op0=ALU.bitwise_and))
            else:
                vop(lambda e: e.tensor_scalar(
                    out=T["u"].ap()[:, 0:wcols].bitcast(I32),
                    in0=T["pk"].ap()[:, 0:wcols], scalar1=4, scalar2=None,
                    op0=ALU.logical_shift_right))
            vop(lambda e: e.tensor_copy(out=T["t1"].ap()[:, 0:wcols],
                                        in_=T["u"].ap()[:, 0:wcols].bitcast(I32)))
            vop(lambda e: e.tensor_scalar(out=T["s8"].ap()[:, 0:wcols],
                                          in0=T["t1"].ap()[:, 0:wcols],
                                          scalar1=8.0, scalar2=None, op0=ALU.is_ge))
            vop(lambda e: e.scalar_tensor_tensor(out=T["w"].ap()[:, 0:wcols],
                                                 in0=T["s8"].ap()[:, 0:wcols],
                                                 scalar=-8.0,
                                                 in1=T["t1"].ap()[:, 0:wcols],
                                                 op0=ALU.mult, op1=ALU.add))
            vop(lambda e: e.tensor_scalar(out=T["e"].ap()[:, 0:wcols],
                                          in0=T["w"].ap()[:, 0:wcols],
                                          scalar1=2.0, scalar2=None, op0=ALU.is_ge))
            vop(lambda e: e.tensor_scalar(out=T["t2"].ap()[:, 0:wcols],
                                          in0=T["w"].ap()[:, 0:wcols],
                                          scalar1=4.0, scalar2=None, op0=ALU.is_ge))
            vop(lambda e: e.tensor_tensor(out=T["e"].ap()[:, 0:wcols],
                                          in0=T["e"].ap()[:, 0:wcols],
                                          in1=T["t2"].ap()[:, 0:wcols], op=ALU.add))
            vop(lambda e: e.tensor_scalar(out=T["t2"].ap()[:, 0:wcols],
                                          in0=T["w"].ap()[:, 0:wcols],
                                          scalar1=6.0, scalar2=None, op0=ALU.is_ge))
            vop(lambda e: e.tensor_tensor(out=T["e"].ap()[:, 0:wcols],
                                          in0=T["e"].ap()[:, 0:wcols],
                                          in1=T["t2"].ap()[:, 0:wcols], op=ALU.add))
            vop(lambda e: e.scalar_tensor_tensor(out=T["f"].ap()[:, 0:wcols],
                                                 in0=T["e"].ap()[:, 0:wcols],
                                                 scalar=-2.0,
                                                 in1=T["w"].ap()[:, 0:wcols],
                                                 op0=ALU.mult, op1=ALU.add))
            aop(lambda e: e.activation(out=T["p2"].ap()[:, 0:wcols],
                                       in_=T["e"].ap()[:, 0:wcols], func=AF.Exp,
                                       scale=LN2))
            vop(lambda e: e.tensor_scalar(out=T["t1"].ap()[:, 0:wcols],
                                          in0=T["f"].ap()[:, 0:wcols],
                                          scalar1=0.25, scalar2=0.5,
                                          op0=ALU.mult, op1=ALU.add))
            vop(lambda e: e.tensor_tensor(out=T["m"].ap()[:, 0:wcols],
                                          in0=T["p2"].ap()[:, 0:wcols],
                                          in1=T["t1"].ap()[:, 0:wcols], op=ALU.mult))
            vop(lambda e: e.tensor_scalar(out=T["t1"].ap()[:, 0:wcols],
                                          in0=T["f"].ap()[:, 0:wcols],
                                          scalar1=-0.25, scalar2=0.5,
                                          op0=ALU.mult, op1=ALU.add))
            vop(lambda e: e.tensor_scalar(out=T["t2"].ap()[:, 0:wcols],
                                          in0=T["s8"].ap()[:, 0:wcols],
                                          scalar1=0.0, scalar2=None, op0=ALU.mult)) \
                if False else None
            vop(lambda e: e.tensor_scalar(out=T["t2"].ap()[:, 0:wcols],
                                          in0=T["w"].ap()[:, 0:wcols],
                                          scalar1=2.0, scalar2=None, op0=ALU.is_lt))
            vop(lambda e: e.tensor_tensor(out=T["t1"].ap()[:, 0:wcols],
                                          in0=T["t1"].ap()[:, 0:wcols],
                                          in1=T["t2"].ap()[:, 0:wcols], op=ALU.mult))
            vop(lambda e: e.tensor_tensor(out=T["m"].ap()[:, 0:wcols],
                                          in0=T["m"].ap()[:, 0:wcols],
                                          in1=T["t1"].ap()[:, 0:wcols],
                                          op=ALU.subtract))
            vop(lambda e: e.tensor_scalar(out=T["sg"].ap()[:, 0:wcols],
                                          in0=T["s8"].ap()[:, 0:wcols],
                                          scalar1=-2.0, scalar2=1.0,
                                          op0=ALU.mult, op1=ALU.add))
            vop(lambda e: e.tensor_tensor(out=T["m"].ap()[:, 0:wcols],
                                          in0=T["m"].ap()[:, 0:wcols],
                                          in1=T["sg"].ap()[:, 0:wcols], op=ALU.mult))
            vop(lambda e, half=half: e.tensor_tensor(
                out=T["dq"].ap()[:, half:2 * wcols:2]
                    .rearrange("p (a b) -> p a b", b=16),
                in0=T["m"].ap()[:, 0:wcols].rearrange("p (a b) -> p a b", b=16),
                in1=T["sc"].ap()[:, 0:wcols // 16].unsqueeze(2)
                    .broadcast_to([128, wcols // 16, 16]),
                op=ALU.mult))
        vop(lambda e: e.tensor_copy(out=T["dqb"].ap()[:, 0:2 * wcols],
                                    in_=T["dq"].ap()[:, 0:2 * wcols]))
        for j in range(trows):
            tmm(lambda e, j=j: e.transpose(pt6[:],
                                           T["dqb"].ap()[:, 128 * j:128 * (j + 1)],
                                           T["idb"][:]))
            vop(lambda e: e.tensor_copy(out=T["tb"][:], in_=pt6[:]))
            E.wait("sync", "vq", E.cnt["vq"])
            E.wait("sync", "st6", E.cnt.get("st6", 0))
            E.bump("st6", 16)
            E.on("sync", lambda e, j=j: e.dma_start(
                out=out_dram[out_row0 + 128 * j:out_row0 + 128 * (j + 1),
                             out_col0:out_col0 + 128],
                in_=T["tb"][:]).then_inc(E.sems["st6"], 16))

    for rt in range(16):
        dequant_unit(gpk[128 * rt:128 * (rt + 1), :], gsc[128 * rt:128 * (rt + 1), :],
                     1024, wgt_s, 0, 128 * rt, 16)
    for rt in range(16):
        dequant_unit(upk[128 * rt:128 * (rt + 1), :], usc[128 * rt:128 * (rt + 1), :],
                     1024, wut_s, 0, 128 * rt, 16)
    for rt in range(2):
        for ck in range(8):
            dequant_unit(dpk[128 * rt:128 * (rt + 1), 1024 * ck:1024 * (ck + 1)],
                         dsc[128 * rt:128 * (rt + 1), 64 * ck:64 * (ck + 1)],
                         1024, wdt_s, 2048 * ck, 128 * rt, 16)
    E.wait("sync", "st6", E.cnt["st6"])
    blk = es.enter_context(nc.Block())
    E.play(blk)
    return nc

FF1 = 16384

def build_a7():
    nc = new_nc()
    r3in = nc.dram_tensor("r3in", [DM, TOK], F32, kind="ExternalInput").ap()
    wgt = nc.dram_tensor("wgt", [DM, FF1], BF16, kind="ExternalInput").ap()
    wut = nc.dram_tensor("wut", [DM, FF1], BF16, kind="ExternalInput").ap()
    wdt = nc.dram_tensor("wdt", [FF1, DM], BF16, kind="ExternalInput").ap()
    ones = nc.dram_tensor("ones", [128, 128], F32, kind="ExternalInput").ap()
    r4f = nc.dram_tensor("r4f", [DM, TOK], F32, kind="ExternalOutput").ap()
    h2_sc = nc.dram_tensor("h2_sc", [FF1, TOK], BF16, kind="Internal").ap()

    es = ExitStack()
    E = Emit(nc, es)
    t = {}
    s_big = es.enter_context(nc.sbuf_tensor("s_big", [128, 128 * 256], BF16))
    t["s_xch"] = s_big
    t["s_w"] = es.enter_context(nc.sbuf_tensor("s_w", [128, 2 * 128 * 128], BF16))
    t["s_xr"] = es.enter_context(nc.sbuf_tensor("s_xr", [128, 2 * 512], F32))
    t["s_s"] = es.enter_context(nc.sbuf_tensor("s_s", [128, 16 * 512], F32))
    t["s_sq"] = es.enter_context(nc.sbuf_tensor("s_sq", [128, 1024], F32))
    t["s_st"] = es.enter_context(nc.sbuf_tensor("s_st", [1, 1536], F32))
    t["s_mr"] = es.enter_context(nc.sbuf_tensor("s_mr", [128, 512], F32))
    t["s_rr"] = es.enter_context(nc.sbuf_tensor("s_rr", [128, 512], F32))
    t["s_on"] = es.enter_context(nc.sbuf_tensor("s_on", [128, 128], F32))
    t["s_out"] = es.enter_context(nc.sbuf_tensor("s_out", [128, 2 * 512], F32))
    t["s_t1"] = es.enter_context(nc.sbuf_tensor("s_t1", [128, 512], F32))
    s_g7 = es.enter_context(nc.sbuf_tensor("s_g7", [128, 512], F32))
    s_hb7 = es.enter_context(nc.sbuf_tensor("s_hb7", [128, 2 * 512], BF16))
    ps0 = es.enter_context(nc.psum_tensor("ps0", [128, 512], F32))
    ps1 = es.enter_context(nc.psum_tensor("ps1", [128, 512], F32))
    ps2 = es.enter_context(nc.psum_tensor("ps2", [128, 512], F32))
    ps3 = es.enter_context(nc.psum_tensor("ps3", [128, 512], F32))
    t["ps"] = [ps0, ps1]
    t["ps_s1"] = es.enter_context(nc.psum_tensor("ps_s1", [1, 512], F32))
    t["ps_s2"] = es.enter_context(nc.psum_tensor("ps_s2", [1, 512], F32))
    t["ps_r"] = es.enter_context(nc.psum_tensor("ps_r", [128, 1024], F32))

    lv = E.bump("ld", 16)
    E.on("sync", lambda e: e.dma_start(out=t["s_on"][:], in_=ones
                                       ).then_inc(E.sems["ld"], 16))
    for eng in ("tensor", "vector", "scalar", "gpsimd"):
        E.wait(eng, "ld", lv)
    for k in range(16):
        E.wait("gpsimd", "f0x", E.cnt.get("f0x", 0))
        E.bump("f0x", 16)
        E.on("gpsimd", lambda e, k=k: e.dma_start(
            out=s_big.ap()[:, TOK * k:TOK * (k + 1)],
            in_=r3in[128 * k:128 * (k + 1), :]).then_inc(E.sems["f0x"], 16))
    for eng in ("tensor",):
        E.wait(eng, "f0x", E.cnt["f0x"])

    psg = [ps0, ps1]
    psu = [ps2, ps3]
    for hm in range(FF1 // 128):
        for wi, wsrc in ((0, wgt), (1, wut)):
            gidx = hm * 2 + wi
            if gidx >= 4:
                E.wait("sync", "f1m", 4 * ((gidx - 4) // 2 + 1))
            E.wait("sync", "f1w", E.cnt.get("f1w", 0))
            wv = E.bump("f1w", 16)
            E.on("sync", lambda e, hm=hm, wsrc=wsrc, gidx=gidx: e.dma_start(
                out=t["s_w"].ap()[:, (gidx % 4) * 2048:(gidx % 4) * 2048 + 2048]
                    .rearrange("p (k c) -> p k c", k=16),
                in_=wsrc[:, 128 * hm:128 * (hm + 1)]
                    .rearrange("(k p) c -> p k c", p=128)).then_inc(E.sems["f1w"], 16))
        for n in range(4):
            gi = hm * 4 + n
            E.wait("tensor", "f1w", 16 * (hm * 2 + 2))
            if gi >= 2:
                E.wait("tensor", "f1e", gi - 1)
            mv = E.bump("f1m")
            E.on("tensor", lambda e, hm=hm, n=n, gi=gi: [
                e.matmul(psg[gi % 2][:],
                         t["s_w"].ap()[:, ((hm * 2) % 4) * 2048 + 128 * k:
                                       ((hm * 2) % 4) * 2048 + 128 * (k + 1)],
                         s_big.ap()[:, k * TOK + 512 * n:k * TOK + 512 * (n + 1)],
                         start=(k == 0), stop=(k == 15)) for k in range(16)][-1])
            E.on("tensor", lambda e, hm=hm, n=n, gi=gi: [
                e.matmul(psu[gi % 2][:],
                         t["s_w"].ap()[:, ((hm * 2 + 1) % 4) * 2048 + 128 * k:
                                       ((hm * 2 + 1) % 4) * 2048 + 128 * (k + 1)],
                         s_big.ap()[:, k * TOK + 512 * n:k * TOK + 512 * (n + 1)],
                         start=(k == 0), stop=(k == 15)) for k in range(16)
            ][-1].then_inc(E.sems["f1m"], 1))
            E.wait("scalar", "f1m", mv)
            sgv = E.bump("f1s")
            E.on("scalar", lambda e, gi=gi: e.activation(
                out=s_g7[:], in_=psg[gi % 2][:],
                func=AF.Gelu_apprx_tanh).then_inc(E.sems["f1s"], 1))
            E.wait("vector", "f1s", sgv)
            if gi >= 2:
                E.wait("vector", "f1st", 16 * (gi - 1))
            ev = E.bump("f1e")
            E.on("vector", lambda e, gi=gi: e.tensor_tensor(
                out=s_hb7.ap()[:, (gi % 2) * 512:(gi % 2) * 512 + 512],
                in0=s_g7[:], in1=psu[gi % 2].ap()[:, :],
                op=ALU.mult).then_inc(E.sems["f1e"], 1))
            E.wait("sync", "f1e", ev)
            E.wait("sync", "f1st", E.cnt.get("f1st", 0))
            E.bump("f1st", 16)
            E.on("sync", lambda e, hm=hm, n=n, gi=gi: e.dma_start(
                out=h2_sc[128 * hm:128 * (hm + 1), 512 * n:512 * (n + 1)],
                in_=s_hb7.ap()[:, (gi % 2) * 512:(gi % 2) * 512 + 512]
            ).then_inc(E.sems["f1st"], 16))
    E.wait("gpsimd", "f1st", E.cnt["f1st"])
    emit_lnproj_phase(E, "f2", wdt, h2_sc, r3in, r4f, KT=128, tiles=t,
                      cast_x=False, prev_e=("f1e", E.cnt["f1e"]), ncol=256)
    E.wait("sync", "f2st", E.cnt["f2st"])
    blk = es.enter_context(nc.Block())
    E.play(blk)
    return nc


def kernel(**inputs):
    f = inputs
    cores = list(range(NCORE))
    im = a1_in_maps(f["L0_attn_x"], f["L0_attn_W_in"], f["L0_attn_conv_w"],
                    f["L0_attn_conv_b"], f["L0_attn_W_x"], f["L0_attn_dt_bias"])
    r = run_bass_kernel_spmd(build_a1(), im, core_ids=cores).results
    xc_full = np.concatenate([r[c]["xc_f"] for c in cores], axis=1)
    zs_full = np.concatenate([r[c]["zs_f"] for c in cores], axis=1)
    dtbct_full = np.concatenate([r[c]["dtbc_t"] for c in cores], axis=1)
    im = a2_in_maps(xc_full, zs_full, dtbct_full, f["L0_attn_A_log"], f["L0_attn_D"])
    r = run_bass_kernel_spmd(build_a2(), im, core_ids=cores).results
    yg_full = np.concatenate([r[c]["ygs"] for c in cores], axis=0)
    del xc_full, zs_full
    im = a3_in_maps(yg_full, f["L0_attn_x"], f["L0_attn_out_W"],
                    f["L0_ffn_w_gate"], f["L0_ffn_w_up"], f["L0_ffn_w_down"])
    r = run_bass_kernel_spmd(build_a3(), im, core_ids=cores).results
    r2_full = np.concatenate([r[c]["r2f"] for c in cores], axis=1)
    del yg_full
    qkv_t = _f32c(np.asarray(f["L1_attn_qkv_w"]).T)
    bgk_t = _f32c(np.concatenate([np.asarray(f["L1_attn_b_w"]),
                                  np.asarray(f["L1_attn_gk_w"])], 0).T)
    cv1 = _f32c(np.asarray(f["L1_attn_conv_w"]))
    ones = np.ones((128, 128), np.float32)
    im = []
    for c in cores:
        t0 = c * TOK
        xh = np.zeros((DM, TOKH), np.float32)
        xh[:, 3:] = r2_full[:, t0:t0 + TOK]
        npv = min(3, t0 % S)
        if npv:
            xh[:, 3 - npv:3] = r2_full[:, t0 - npv:t0]
        im.append(dict(r2h=xh, qkv_t=qkv_t, bgk_t=bgk_t, cv1=cv1, ones=ones))
    r = run_bass_kernel_spmd(build_a4(), im, core_ids=cores).results
    qn_full = np.concatenate([r[c]["qn"] for c in cores], axis=1)
    kn_full = np.concatenate([r[c]["kn"] for c in cores], axis=1)
    vv_full = np.concatenate([r[c]["vv"] for c in cores], axis=1)
    bg_full = np.concatenate([r[c]["beta_g"] for c in cores], axis=1)
    tri = np.ascontiguousarray(np.triu(np.ones((128, 128), np.float32)))
    mus = np.ascontiguousarray(np.triu(np.ones((128, 128), np.float32), 1))
    mls = np.ascontiguousarray(np.tril(np.ones((128, 128), np.float32), -1))
    mui = np.ascontiguousarray(np.triu(np.ones((128, 128), np.float32), 0))
    idf = np.eye(128, dtype=np.float32)
    im = []
    for c in cores:
        q4 = np.zeros((4 * 128, S), np.float32)
        k4 = np.zeros((4 * 128, S), np.float32)
        v4 = np.zeros((4 * 128, S), np.float32)
        bg4 = np.zeros((8, S), np.float32)
        for i in range(4):
            bh = c * 4 + i
            b, h = bh // 16, bh % 16
            q4[i * 128:(i + 1) * 128] = qn_full[128 * h:128 * (h + 1), b * S:(b + 1) * S]
            k4[i * 128:(i + 1) * 128] = kn_full[128 * (h // 4):128 * (h // 4 + 1),
                                                b * S:(b + 1) * S]
            v4[i * 128:(i + 1) * 128] = vv_full[128 * (h // 4):128 * (h // 4 + 1),
                                                b * S:(b + 1) * S]
            bg4[2 * i] = bg_full[h, b * S:(b + 1) * S]
            bg4[2 * i + 1] = bg_full[16 + h, b * S:(b + 1) * S]
        im.append(dict(qf4=q4, kf4=k4, vf4=v4, bg4=bg4, tri=tri, mus=mus,
                       mls=mls, mui=mui, idf=idf))
    r = run_bass_kernel_spmd(build_a5a(), im, core_ids=cores).results
    o_t = np.zeros((B * S, 16, 128), np.float32)
    for c in cores:
        for i in range(4):
            bh = c * 4 + i
            b, h = bh // 16, bh % 16
            o_t[b * S:(b + 1) * S, h] = r[c]["o4"][i * S:(i + 1) * S]
    o_f_full = np.ascontiguousarray(o_t.reshape(B * S, 2048).T)
    del qn_full, kn_full, vv_full, o_t
    g_w_t = _f32c(np.asarray(f["L1_attn_g_w"]).T)
    o_w_t = _f32c(np.asarray(f["L1_attn_o_w"]).T)
    onw = _f32c(np.asarray(f["L1_attn_out_norm_w"]).reshape(128, 1))
    im = []
    for c in cores:
        t0 = c * TOK
        im.append(dict(o_f=_f32c(o_f_full[:, t0:t0 + TOK]),
                       r2f_in=_f32c(r2_full[:, t0:t0 + TOK]),
                       g_w_t=g_w_t, o_w_t=o_w_t, onw=onw, ones=ones))
    r = run_bass_kernel_spmd(build_a5b(), im, core_ids=cores).results
    r3_full = np.concatenate([r[c]["r3f"] for c in cores], axis=1)
    del o_f_full, r2_full
    im = []
    gpk_a = np.asarray(f["L1_ffn_gate_W_packed"])
    gsc_a = _f32c(f["L1_ffn_gate_scales"])
    upk_a = np.asarray(f["L1_ffn_up_W_packed"])
    usc_a = _f32c(f["L1_ffn_up_scales"])
    dpk_a = np.asarray(f["L1_ffn_down_W_packed"])
    dsc_a = _f32c(f["L1_ffn_down_scales"])
    for c in cores:
        im.append(dict(
            gpk=np.ascontiguousarray(gpk_a[2048 * c:2048 * (c + 1)], dtype=np.int32),
            gsc=_f32c(gsc_a[2048 * c:2048 * (c + 1)]),
            upk=np.ascontiguousarray(upk_a[2048 * c:2048 * (c + 1)], dtype=np.int32),
            usc=_f32c(usc_a[2048 * c:2048 * (c + 1)]),
            dpk=np.ascontiguousarray(dpk_a[256 * c:256 * (c + 1)], dtype=np.int32),
            dsc=_f32c(dsc_a[256 * c:256 * (c + 1)]),
            idf=idf))
    r = run_bass_kernel_spmd(build_a6(), im, core_ids=cores).results
    wgt_full = np.concatenate([r[c]["wgt_s"] for c in cores], axis=1)
    wut_full = np.concatenate([r[c]["wut_s"] for c in cores], axis=1)
    wdt_full = np.concatenate([r[c]["wdt_s"] for c in cores], axis=1)
    im = []
    for c in cores:
        t0 = c * TOK
        im.append(dict(r3in=_f32c(r3_full[:, t0:t0 + TOK]),
                       wgt=wgt_full, wut=wut_full, wdt=wdt_full, ones=ones))
    r = run_bass_kernel_spmd(build_a7(), im, core_ids=cores).results
    r4_full = np.concatenate([r[c]["r4f"] for c in cores], axis=1)
    out = np.ascontiguousarray(r4_full.T.reshape(B, S, DM), dtype=np.float32)
    return out
